# version 08
# baseline (speedup 1.0000x reference)
import math
from contextlib import ExitStack

import numpy as np
import ml_dtypes

import concourse.bass as bass
import concourse.mybir as mybir
from concourse.bass_utils import run_bass_kernel_spmd

F32 = mybir.dt.float32
BF16 = mybir.dt.bfloat16
AF = mybir.ActivationFunctionType
ALU = mybir.AluOpType
bfdt = ml_dtypes.bfloat16

B, L, D, NH, HD, FF = 2, 8192, 2048, 16, 128, 8192
NT = 2048
TTS = 512
NTT = NT // TTS
NDC = D // 128
NOC = 16
NHC = FF // 128
EPS = 1e-5
LOG_RSHD = float(np.log(1.0 / np.sqrt(HD)))
W2SCALE = float(1.0 / np.sqrt(FF))
ENGS = ("sync", "gpsimd", "tensor", "vector", "scalar")


def _had(n):
    h = np.array([[1.0]], np.float32)
    while h.shape[0] < n:
        h = np.block([[h, h], [h, -h]])
    return h


class KB:

    def __init__(self, nc, st):
        self.nc = nc
        self.st = st
        self.ops = {e: [] for e in ENGS}
        self.sems = {}
        self.cnt = {}

    def mksem(self, *names):
        for n in names:
            if n not in self.sems:
                self.sems[n] = self.st.enter_context(self.nc.semaphore(n))
                self.cnt[n] = 0

    def w(self, eng, tok, future=0):
        if tok is None:
            return
        s, v = tok
        assert v <= self.cnt[s] + future, (eng, s, v, self.cnt[s])
        h = self.sems[s]
        self.ops[eng].append(lambda e, h=h, v=v: e.wait_ge(h, v))

    def op(self, eng, f, inc=None, amt=1):
        if inc is None:
            self.ops[eng].append(f)
            return None
        self.cnt[inc] += amt
        tok = (inc, self.cnt[inc])
        h = self.sems[inc]

        def g(e, f=f, h=h, a=amt):
            f(e).then_inc(h, a)
        self.ops[eng].append(g)
        return tok

    def dma(self, qeng, slot, dst, src, waits=(), **kw):
        for t in waits:
            self.w(qeng, t)
        sem = "d_" + slot
        self.mksem(sem)
        return self.op(qeng, lambda e, d=dst, s=src, kw=kw: e.dma_start(out=d, in_=s, **kw),
                       inc=sem, amt=16)

    def tot(self, slot):
        return ("d_" + slot, self.cnt["d_" + slot])

    def echo(self):
        return self.op("vector", lambda e: e.tensor_copy(
            self.escr[0:1, 0:1], self.escr[0:1, 1:2]), inc="sv")

    def barrier(self):
        base = self.cnt["sb"]
        for e in ENGS:
            self.op(e, lambda en: en.nop(), inc="sb")
        toks = [("sb", base + len(ENGS))]
        toks += [(s, self.cnt[s]) for s in self.sems if s.startswith("d_")]
        for e in ENGS:
            for t in toks:
                self.w(e, t)


def build(dbg=False, stages=5, p1a=True, p1m=3, p1post=3):
    nc = bass.Bass("TRN2", target_bir_lowering=False, debug=False)
    DBG = "ExternalOutput" if dbg else "Internal"

    x = nc.dram_tensor("x", [D, NT], F32, kind="ExternalInput").ap()
    y = nc.dram_tensor("y", [D, NT], F32, kind="ExternalOutput").ap()
    wq = nc.dram_tensor("wq", [NOC, 128, D], F32, kind="ExternalInput").ap()
    wk = nc.dram_tensor("wk", [NOC, 128, D], F32, kind="ExternalInput").ap()
    wv = nc.dram_tensor("wv", [NOC, 128, D], F32, kind="ExternalInput").ap()
    wo = nc.dram_tensor("wo", [NOC, 128, D], F32, kind="ExternalInput").ap()
    w1 = nc.dram_tensor("w1", [NHC, 128, D], F32, kind="ExternalInput").ap()
    w3 = nc.dram_tensor("w3", [NHC, 128, D], F32, kind="ExternalInput").ap()
    w2 = nc.dram_tensor("w2", [64, 128, D], F32, kind="ExternalInput").ap()
    bo = nc.dram_tensor("bo", [128, NOC], F32, kind="ExternalInput").ap()

    cc_qk_in = nc.dram_tensor("cc_qk_in", [2 * D, NT], BF16, kind="Internal").ap()
    cc_qk_out = nc.dram_tensor("cc_qk_out", [8 * 2 * D, NT], BF16, kind="Internal").ap()
    cc_v_in = nc.dram_tensor("cc_v_in", [D, 16 * 129], BF16, kind="Internal").ap()
    cc_vout = nc.dram_tensor("cc_vout", [8 * D, 16 * 129], BF16, kind="Internal").ap()
    cc_o_in = nc.dram_tensor("cc_o_in", [512, L], BF16, kind="Internal").ap()
    cc_o_out = nc.dram_tensor("cc_o_out", [8 * 512, L], BF16, kind="Internal").ap()
    my_q = nc.dram_tensor("my_q", [256, 2 * L], BF16, kind=DBG).ap()
    my_k = nc.dram_tensor("my_k", [256, 2 * L], BF16, kind=DBG).ap()
    my_v = nc.dram_tensor("my_v", [256, 2 * 64 * 129], BF16, kind=DBG).ap()
    my_o = nc.dram_tensor("my_o", [D, NT], BF16, kind=DBG).ap()
    w2m = nc.dram_tensor("w2m", [64, 128 * D], BF16, kind=DBG).ap()
    w2qd = nc.dram_tensor("w2qd", [NOC, 128, 64 * 128], BF16, kind=DBG).ap()
    x1d = nc.dram_tensor("x1d", [D, NT], F32, kind=DBG).ap()
    hidd = nc.dram_tensor("hidd", [FF, NT], BF16, kind=DBG).ap()

    c_ident = nc.inline_tensor(np.eye(128, dtype=bfdt), name="c_ident").ap()
    c_h64 = nc.inline_tensor(_had(64).astype(bfdt), name="c_h64").ap()
    c_h128 = nc.inline_tensor(_had(128).astype(bfdt), name="c_h128").ap()
    tri = (np.arange(128)[:, None] <= np.arange(128)[None, :]).astype(bfdt)
    c_tri = nc.inline_tensor(tri, name="c_tri").ap()
    c_onesc = nc.inline_tensor(np.ones((128, 1), np.float32), name="c_onesc").ap()
    c_onesr = nc.inline_tensor(np.ones((1, 128), np.float32), name="c_onesr").ap()

    with ExitStack() as st:
        E = st.enter_context
        kb = KB(nc, st)
        kb.mksem("scc", "sp", "sv", "ss", "sb")

        ident = E(nc.sbuf_tensor("ident", [128, 128], BF16))
        h64s = E(nc.sbuf_tensor("h64s", [64, 64], BF16))
        h128s = E(nc.sbuf_tensor("h128s", [128, 128], BF16))
        tris = E(nc.sbuf_tensor("tris", [128, 128], BF16))
        onesc = E(nc.sbuf_tensor("onesc", [128, 1], F32))
        onesr = E(nc.sbuf_tensor("onesr", [1, 128], F32))
        bos = E(nc.sbuf_tensor("bos", [128, NOC], F32))
        r_mu = E(nc.sbuf_tensor("r_mu", [1, 512], F32))
        r_mu2 = E(nc.sbuf_tensor("r_mu2", [1, 512], F32))
        r_var = E(nc.sbuf_tensor("r_var", [1, 512], F32))
        r_lnv = E(nc.sbuf_tensor("r_lnv", [1, 512], F32))
        r_ab = E(nc.sbuf_tensor("r_ab", [1, 1024], F32))
        ev = [E(nc.sbuf_tensor(f"ev{i}", [128, 512], F32)) for i in range(2)]
        sq = [E(nc.sbuf_tensor(f"sq{i}", [128, 512], F32)) for i in range(2)]
        tmp = [E(nc.sbuf_tensor(f"tmp{i}", [128, 512], F32)) for i in range(2)]
        qkst = [E(nc.sbuf_tensor(f"qkst{i}", [128, NT], BF16)) for i in range(2)]
        vst = [E(nc.sbuf_tensor(f"vst{i}", [128, 16 * 129], BF16)) for i in range(2)]
        pt = [E(nc.sbuf_tensor(f"pt{i}", [128, 512], BF16)) for i in range(2)]
        otm = [E(nc.sbuf_tensor(f"otm{i}", [128, 128], BF16)) for i in range(2)]
        rcs = E(nc.sbuf_tensor("rcs", [128, 4], F32))
        kb_escr = E(nc.sbuf_tensor("kb_escr", [1, 2], F32))
        x1c = [E(nc.sbuf_tensor(f"x1c{i}", [128, 512], F32)) for i in range(2)]
        yst = [E(nc.sbuf_tensor(f"yst{i}", [128, 512], F32)) for i in range(2)]

        P = [E(nc.psum_tensor(f"P{i}", [128, 512], F32)) for i in range(6)]
        TPBS = [E(nc.psum_tensor(f"TPB{i}", [128, 128], BF16)) for i in range(2)]

        kb.escr = kb_escr
        for dst, src in ((ident, c_ident), (h64s, c_h64), (h128s, c_h128), (tris, c_tri),
                         (onesc, c_onesc), (onesr, c_onesr), (bos, bo)):
            kb.dma("sync", "cst", dst[:, :], src)
        t_init = kb.tot("cst")
        kb.op("vector", lambda e: e.memset(kb_escr[0:1, :], 0.0))
        for i in range(2):
            tok = kb.op("vector", lambda e, b=vst[i]: e.memset(b[:, :], 1.0), inc="sv")
        t_vst_ones = tok
        for e in ENGS:
            kb.w(e, t_init)
        kb.w("scalar", t_vst_ones)
        kb.w("sync", t_vst_ones)

        def rows_ln(stat_a, stat_b, nfeat, exp_bias, want_b):
            inv = 1.0 / nfeat
            kb.op("vector", lambda e: e.tensor_scalar_mul(r_mu[0:1, :], stat_a[0:1, 0:512], inv))
            kb.op("vector", lambda e: e.tensor_mul(r_mu2[0:1, :], r_mu[0:1, :], r_mu[0:1, :]))
            kb.op("vector", lambda e: e.scalar_tensor_tensor(
                r_var[0:1, :], stat_b[0:1, 0:512], inv, r_mu2[0:1, :], ALU.mult, ALU.subtract))
            kb.op("vector", lambda e: e.tensor_scalar_add(
                r_var[0:1, :], r_var[0:1, :], EPS))
            tv = kb.echo()
            kb.w("scalar", tv)
            tl = kb.op("scalar", lambda e: e.activation(r_lnv[0:1, :], r_var[0:1, :], AF.Ln),
                       inc="ss")
            kb.w("vector", tl)
            if exp_bias != 0.0:
                kb.op("vector", lambda e, bb=exp_bias: e.tensor_scalar(
                    r_lnv[0:1, :], r_lnv[0:1, :], -0.5, bb, ALU.mult, ALU.add))
            else:
                kb.op("vector", lambda e: e.tensor_scalar_mul(
                    r_lnv[0:1, :], r_lnv[0:1, :], -0.5))
            tm = kb.echo()
            kb.w("scalar", tm)
            ta = kb.op("scalar", lambda e: e.activation(
                r_ab[0:1, 0:512], r_lnv[0:1, :], AF.Exp), inc="ss")
            if want_b:
                kb.w("vector", ta)
                kb.op("vector", lambda e: e.tensor_mul(r_ab[0:1, 512:1024], r_mu[0:1, :],
                                                       r_ab[0:1, 0:512]))
                kb.op("vector", lambda e: e.tensor_scalar_mul(
                    r_ab[0:1, 512:1024], r_ab[0:1, 512:1024], -1.0))
                tb = kb.echo()
                return ta, tb
            return ta, None

        with ExitStack() as sa:
            ln_sb = sa.enter_context(nc.sbuf_tensor("ln_sb", [128, NDC * NT], BF16))
            xt = [sa.enter_context(nc.sbuf_tensor(f"xt{i}", [128, NDC * TTS], F32))
                  for i in range(2)]
            wsl = [sa.enter_context(nc.sbuf_tensor(f"wsl{i}", [128, D], BF16))
                   for i in range(2)]

            x3 = x.rearrange("(dc p) t -> p dc t", p=128)
            ln_ready = [None] * NTT
            xt_free = [None, None]
            rows_fr = [None]
            p_free = [None] * 6
            tpb_free = None
            ev_free = [None, None]
            sq_free = [None, None]
            pt_free = [None, None]
            tmp_free = [None, None]

            for i in range(NTT if p1a else 0):
                ip = i % 2
                t_x = kb.dma("sync", f"xt{ip}", xt[ip].rearrange("p (dc t) -> p dc t", t=TTS),
                             x3[:, :, i * TTS:(i + 1) * TTS], waits=[xt_free[ip]])
                kb.w("tensor", t_x)
                kb.w("tensor", p_free[0])
                kb.w("tensor", p_free[1])
                kb.w("vector", t_x)
                sq_toks = []
                for dc in range(NDC):
                    xs = xt[ip][:, dc * TTS:(dc + 1) * TTS]
                    kb.op("tensor", lambda e, o=P[0][0:1, 0:512], l=onesc[:, 0:1], r=xs,
                          s=(dc == 0), t=(dc == NDC - 1): e.matmul(o, l, r, start=s, stop=t))
                    dp = dc % 2
                    if dc >= 2:
                        kb.w("vector", sq_toks[dc - 2][1])
                    kb.op("vector", lambda e, o=sq[dp][:, :], a=xs: e.tensor_mul(o, a, a))
                    tsq = kb.echo()
                    kb.w("tensor", tsq)
                    tmm = kb.op("tensor", lambda e, o=P[1][0:1, 0:512], l=onesc[:, 0:1],
                                r=sq[dp][:, :], s=(dc == 0), t=(dc == NDC - 1):
                                e.matmul(o, l, r, start=s, stop=t), inc="sp")
                    sq_toks.append((tsq, tmm))
                t_stats = sq_toks[-1][1]
                kb.w("vector", t_stats)
                kb.w("vector", rows_fr[0])
                ta, tb = rows_ln(P[0], P[1], D, 0.0, True)
                kb.w("tensor", tb)
                kb.w("tensor", p_free[2])
                kb.w("tensor", p_free[3])
                kb.op("tensor", lambda e: e.matmul(P[2][:, 0:512], onesr[0:1, :],
                                                   r_ab[0:1, 0:512], start=True, stop=True))
                t_bc = kb.op("tensor", lambda e: e.matmul(P[3][:, 0:512], onesr[0:1, :],
                                                          r_ab[0:1, 512:1024], start=True,
                                                          stop=True), inc="sp")
                rows_fr[0] = t_bc
                p_free[0] = t_bc
                p_free[1] = t_bc
                kb.w("vector", t_bc)
                t_ap = None
                for dc in range(NDC):
                    xs = xt[ip][:, dc * TTS:(dc + 1) * TTS]
                    dp = dc % 2
                    if dc >= 2:
                        kb.w("vector", tmp_free[dp])
                    kb.op("vector", lambda e, o=tmp[dp][:, :], a=xs: e.tensor_mul(o, a, P[2][:, 0:512]))
                    dst = ln_sb[:, dc * NT + i * TTS: dc * NT + (i + 1) * TTS]
                    t_ap = kb.op("vector", lambda e, o=dst, a=tmp[dp][:, :]:
                                 e.tensor_add(o, a, P[3][:, 0:512]), inc="sv")
                    tmp_free[dp] = t_ap
                ln_ready[i] = kb.echo()
                xt_free[ip] = t_ap
                p_free[2] = ("sv", t_ap[1] + 1)
                p_free[3] = ("sv", t_ap[1] + 1)

            wt_free = [None, None]
            qkst_free = [None, None]
            vst_free = [None, None]
            tpb_free = [None, None]
            gtile = 0
            t_cc_qk = [None]
            t_cc_v = [None]

            def post_1b(ctx):
                m, oc, i, ip, pi, t_p, op = ctx
                PP = P[pi]
                if m < 2:
                    kb.w("scalar", t_p)
                    kb.w("scalar", ev_free[ip])
                    t_ev = kb.op("scalar", lambda e, o=ev[ip][:, :], s=PP[:, 0:512]:
                                 e.copy(o, s), inc="ss")
                    p_free[pi] = ("ss", t_ev[1] + 1)
                    kb.w("vector", t_ev)
                    kb.w("vector", sq_free[ip])
                    kb.op("vector", lambda e, o=sq[ip][:, :], a=ev[ip][:, :]:
                          e.tensor_mul(o, a, a))
                    t_sq = kb.echo()
                    kb.w("tensor", t_ev)
                    kb.w("tensor", p_free[2])
                    kb.w("tensor", p_free[3])
                    kb.op("tensor", lambda e, r=ev[ip][:, :]: e.matmul(
                        P[2][0:1, 0:512], onesc[:, 0:1], r, start=True, stop=True))
                    kb.w("tensor", t_sq)
                    t_st = kb.op("tensor", lambda e, r=sq[ip][:, :]: e.matmul(
                        P[3][0:1, 0:512], onesc[:, 0:1], r, start=True, stop=True),
                        inc="sp")
                    sq_free[ip] = t_st
                    kb.w("vector", t_st)
                    kb.w("vector", rows_fr[0])
                    ta, tb = rows_ln(P[2], P[3], HD, LOG_RSHD if m == 0 else 0.0,
                                     m == 1)
                    kb.w("tensor", tb if m == 1 else ta)
                    kb.w("tensor", p_free[4])
                    kb.w("tensor", p_free[5] if m == 1 else None)
                    if m == 1:
                        kb.op("tensor", lambda e: e.matmul(
                            P[4][:, 0:512], onesr[0:1, :], r_ab[0:1, 0:512],
                            start=True, stop=True))
                        t_bc = kb.op("tensor", lambda e: e.matmul(
                            P[5][:, 0:512], onesr[0:1, :], r_ab[0:1, 512:1024],
                            start=True, stop=True), inc="sp")
                    else:
                        t_bc = kb.op("tensor", lambda e: e.matmul(
                            P[4][:, 0:512], onesr[0:1, :], r_ab[0:1, 0:512],
                            start=True, stop=True), inc="sp")
                    rows_fr[0] = t_bc
                    p_free[2] = t_bc
                    p_free[3] = t_bc
                    kb.w("vector", t_bc)
                    kb.w("vector", qkst_free[op])
                    dst = qkst[op][:, i * TTS:(i + 1) * TTS]
                    if m == 0:
                        t_sg = kb.op("vector", lambda e, o=dst, a=ev[ip][:, :]:
                                     e.tensor_mul(o, a, P[4][:, 0:512]), inc="sv")
                    else:
                        kb.w("vector", tmp_free[0])
                        kb.op("vector", lambda e, a=ev[ip][:, :]:
                              e.tensor_mul(tmp[0][:, :], a, P[4][:, 0:512]))
                        t_sg = kb.op("vector", lambda e, o=dst:
                                     e.tensor_add(o, tmp[0][:, :], P[5][:, 0:512]),
                                     inc="sv")
                        tmp_free[0] = t_sg
                    ev_free[ip] = ("sv", t_sg[1] + 1)
                    p_free[4] = ("sv", t_sg[1] + 1)
                    p_free[5] = ("sv", t_sg[1] + 1)
                    t_last = t_sg
                else:
                    kb.w("scalar", t_p)
                    kb.w("scalar", pt_free[ip])
                    t_vb = kb.op("scalar", lambda e, o=pt[ip][:, :], s=PP[:, 0:512]:
                                 e.copy(o, s), inc="ss")
                    p_free[pi] = ("ss", t_vb[1] + 1)
                    for qb in range(4):
                        tb_ = qb % 2
                        kb.w("tensor", t_vb)
                        kb.w("tensor", tpb_free[tb_], future=1)
                        t_tp = kb.op("tensor", lambda e, s=pt[ip][:, qb * 128:(qb + 1) * 128],
                                     o=TPBS[tb_][:, :]:
                                     e.transpose(o, s, ident[:, :]), inc="sp")
                        if qb == 3:
                            pt_free[ip] = t_tp
                        kb.w("scalar", t_tp)
                        kb.w("scalar", vst_free[op])
                        col = (i * 4 + qb) * 129
                        t_te = kb.op("scalar", lambda e, o=vst[op][:, col:col + 128],
                                     s=TPBS[tb_][:, :]: e.copy(o, s), inc="ss")
                        tpb_free[tb_] = ("ss", t_te[1] + 1)
                        t_last = t_te
                if i == NTT - 1:
                    if m == 0:
                        dst = cc_qk_in[oc * 128:(oc + 1) * 128, :]
                        t_out = kb.dma("sync", f"qs{op}", dst, qkst[op][:, :], waits=[t_last])
                        qkst_free[op] = t_out
                    elif m == 1:
                        dst = cc_qk_in[D + oc * 128: D + (oc + 1) * 128, :]
                        t_out = kb.dma("sync", f"qs{op}", dst, qkst[op][:, :], waits=[t_last])
                        qkst_free[op] = t_out
                    else:
                        dst = cc_v_in[oc * 128:(oc + 1) * 128, :]
                        t_out = kb.dma("sync", f"vs{op}", dst, vst[op][:, :], waits=[t_last])
                        vst_free[op] = t_out

            pending = None
            for m, wten in enumerate((wq, wk, wv)[:p1m]):
                for oc in range(NOC):
                    wp = oc % 2
                    t_w = kb.dma("gpsimd", f"ws{wp}", wsl[wp][:, :], wten[oc], waits=[wt_free[wp]])
                    op = oc % 2
                    for i in range(NTT):
                        ip = i % 2
                        pi = gtile % 2
                        gtile += 1
                        PP = P[pi]
                        kb.w("tensor", t_w)
                        kb.w("tensor", ln_ready[i])
                        kb.w("tensor", p_free[pi], future=2)
                        for dc in range(NDC):
                            t_p = kb.op("tensor", lambda e, o=PP[:, 0:512],
                                        l=wsl[wp][:, dc * 128:(dc + 1) * 128],
                                        r=ln_sb[:, dc * NT + i * TTS: dc * NT + (i + 1) * TTS],
                                        s=(dc == 0), t=(dc == NDC - 1):
                                        e.matmul(o, l, r, start=s, stop=t),
                                        inc="sp" if dc == NDC - 1 else None)
                        wt_free[wp] = t_p
                        cur = (m, oc, i, ip, pi, t_p, op)
                        if pending is not None:
                            post_1b(pending)
                        pending = cur
                if m == 1 and oc == NOC - 1:
                    post_1b(pending)
                    pending = None
                    kb.w("gpsimd", kb.tot("qs0"))
                    kb.w("gpsimd", kb.tot("qs1"))
                    t_cc_qk[0] = kb.op("gpsimd", lambda e: e.collective_compute(
                        "AllGather", ALU.bypass,
                        replica_groups=[[0, 1, 2, 3, 4, 5, 6, 7]],
                        ins=[cc_qk_in], outs=[cc_qk_out]), inc="scc")
            post_1b(pending)
            kb.w("gpsimd", kb.tot("vs0"))
            kb.w("gpsimd", kb.tot("vs1"))
            t_cc_v[0] = kb.op("gpsimd", lambda e: e.collective_compute(
                "AllGather", ALU.bypass, replica_groups=[[0, 1, 2, 3, 4, 5, 6, 7]],
                ins=[cc_v_in], outs=[cc_vout]), inc="scc")

            kb.mksem("d_qs0", "d_qs1", "d_vs0", "d_vs1")
            t_p1_dma = [kb.tot("qs0"), kb.tot("qs1"), kb.tot("vs0"), kb.tot("vs1")]

            kb.barrier()

        if stages >= 2:
            for t in t_p1_dma:
                kb.w("gpsimd", t)
            t_cc_qk = kb.op("gpsimd", lambda e: e.collective_compute(
                "AllGather", ALU.bypass, replica_groups=[[0, 1, 2, 3, 4, 5, 6, 7]],
                ins=[cc_qk_in], outs=[cc_qk_out]), inc="scc")
            t_cc_v = kb.op("gpsimd", lambda e: e.collective_compute(
                "AllGather", ALU.bypass, replica_groups=[[0, 1, 2, 3, 4, 5, 6, 7]],
                ins=[cc_v_in], outs=[cc_vout]), inc="scc")

            with ExitStack() as sb:
                w2sl = [sb.enter_context(nc.sbuf_tensor(f"w2sl{i}", [64, 8192], BF16))
                        for i in range(2)]
                f1st = sb.enter_context(nc.sbuf_tensor("f1st", [64, 8192], BF16))
                w2f = [sb.enter_context(nc.sbuf_tensor(f"w2f{i}", [128, D], BF16))
                       for i in range(2)]
                f2st = sb.enter_context(nc.sbuf_tensor("f2st", [128, D], BF16))
                w2flat = w2.rearrange("a b d -> a (b d)")

                sl_free = [None, None]
                p_free = [None, None]
                f1_free = None
                for s in range(32):
                    sp_ = s % 2
                    t_w = kb.dma("gpsimd", f"fw{sp_}", w2sl[sp_][:, :],
                                 w2flat[:, s * 8192:(s + 1) * 8192], waits=[sl_free[sp_]])
                    seg_toks = []
                    for n in range(16):
                        pi = n % 2
                        kb.w("tensor", t_w)
                        kb.w("tensor", p_free[pi])
                        t_mm = kb.op("tensor", lambda e, o=P[pi][0:64, 0:512],
                                     r=w2sl[sp_][:, n * 512:(n + 1) * 512]:
                                     e.matmul(o, h64s[0:64, 0:64], r, start=True, stop=True),
                                     inc="sp")
                        kb.w("scalar", t_mm)
                        kb.w("scalar", f1_free if n == 0 else None)
                        t_ev = kb.op("scalar", lambda e, o=f1st[:, n * 512:(n + 1) * 512],
                                     s2=P[pi][0:64, 0:512]:
                                     e.copy(o, s2), inc="ss")
                        p_free[pi] = t_ev
                        seg_toks.append(t_ev)
                    sl_free[sp_] = seg_toks[-1]
                    f1_free = kb.dma("sync", "f1w", w2m[:, s * 8192:(s + 1) * 8192], f1st[:, :],
                                     waits=[seg_toks[-1]])
                t_w2m = kb.tot("f1w")
                f2_free = None
                wf_free = [None, None]
                for a in range(64):
                    ap_ = a % 2
                    t_w = kb.dma("sync", f"fl{ap_}", w2f[ap_][:, :],
                                 w2m[a:a + 1, :].rearrange("o (b d) -> (o b) d", b=128),
                                 waits=[wf_free[ap_], t_w2m if a == 0 else None])
                    seg_toks = []
                    for n in range(4):
                        pi = n % 2
                        kb.w("tensor", t_w)
                        kb.w("tensor", p_free[pi])
                        t_mm = kb.op("tensor", lambda e, o=P[pi][:, 0:512],
                                     r=w2f[ap_][:, n * 512:(n + 1) * 512]:
                                     e.matmul(o, h128s[:, :], r, start=True, stop=True), inc="sp")
                        kb.w("scalar", t_mm)
                        kb.w("scalar", f2_free if n == 0 else None)
                        t_ev = kb.op("scalar", lambda e, o=f2st[:, n * 512:(n + 1) * 512],
                                     s2=P[pi][:, 0:512]: e.copy(o, s2), inc="ss")
                        p_free[pi] = ("ss", t_ev[1] + 1)
                        seg_toks.append(t_ev)
                    wf_free[ap_] = seg_toks[-1]
                    f2_free = kb.dma(
                        "sync", "f2w",
                        w2qd.rearrange("oc b f -> b oc f")[:, :, a * 128:(a + 1) * 128],
                        f2st.rearrange("b (oc o) -> b oc o", oc=16),
                        waits=[seg_toks[-1]])
                t_w2q = kb.tot("f2w")

                kb.w("sync", t_cc_qk)
                kb.w("sync", t_cc_v)
                qko = cc_qk_out.rearrange("(r hr) t -> r hr t", hr=2 * D)
                vo = cc_vout.rearrange("(r hr) f -> r hr f", hr=D)
                mq = my_q.rearrange("(hl p) (b j t) -> hl b j p t", hl=2, b=2, j=4)
                mk = my_k.rearrange("(hl p) (b j t) -> hl b j p t", hl=2, b=2, j=4)
                mv = my_v.rearrange("(hl p) (b j f) -> hl b j p f", hl=2, b=2, j=4)

                kb.mksem("d_cpy")

                def copy_block(e):
                    for arm in e.switch_core_id(n=128):
                        c = arm.logical % 8
                        for hl in range(2):
                            h = 2 * c + hl
                            for b in range(2):
                                e.dma_start(out=mq[hl, b], in_=qko[4 * b:4 * b + 4,
                                            h * 128:(h + 1) * 128, :]).then_inc(kb.sems["d_cpy"], 16)
                                e.dma_start(out=mk[hl, b], in_=qko[4 * b:4 * b + 4,
                                            D + h * 128: D + (h + 1) * 128, :]).then_inc(kb.sems["d_cpy"], 16)
                                e.dma_start(out=mv[hl, b], in_=vo[4 * b:4 * b + 4,
                                            h * 128:(h + 1) * 128, :]).then_inc(kb.sems["d_cpy"], 16)
                kb.cnt["d_cpy"] += 12 * 16
                kb.ops["sync"].append(copy_block)
                t_copy = kb.tot("cpy")

                kb.barrier()

            if stages >= 3:
                with ExitStack() as sc:
                    kbuf = [sc.enter_context(nc.sbuf_tensor(f"kbuf{i}", [128, L], BF16))
                            for i in range(2)]
                    vbuf = [sc.enter_context(nc.sbuf_tensor(f"vbuf{i}", [128, 64 * 129], BF16))
                            for i in range(2)]
                    qbuf = [sc.enter_context(nc.sbuf_tensor(f"qbuf{i}", [128, L], BF16))
                            for i in range(2)]
                    osta = [sc.enter_context(nc.sbuf_tensor(f"osta{i}", [128, L], BF16))
                            for i in range(2)]

                    pair_cons = [None] * 4
                    osta_free = [None, None]
                    ld_toks = [None] * 4

                    def emit_pair_load(p):
                        hl, b = p // 2, p % 2
                        bp = p % 2
                        w_ = [pair_cons[p - 2] if p >= 2 else None]
                        t1 = kb.dma("sync", f"kb{bp}", kbuf[bp][:, :],
                                    my_k[hl * 128:(hl + 1) * 128, b * L:(b + 1) * L], waits=w_)
                        t2 = kb.dma("sync", f"vb{bp}", vbuf[bp][:, :],
                                    my_v[hl * 128:(hl + 1) * 128, b * 8256:(b + 1) * 8256])
                        t3 = kb.dma("sync", f"qb{bp}", qbuf[bp][:, :],
                                    my_q[hl * 128:(hl + 1) * 128, b * L:(b + 1) * L])
                        ld_toks[p] = (t1, t2, t3)

                    kb.w("sync", t_copy)
                    emit_pair_load(0)
                    emit_pair_load(1)

                    pst_free = [None, None]
                    pt_free = [None, None]
                    acc_free = [None] * 4
                    tpb_free = [None, None]
                    otm_free = [None, None]

                    for p in range(4):
                        hl, b = p // 2, p % 2
                        bp = p % 2
                        for t in ld_toks[p]:
                            kb.w("tensor", t)
                        t_pe_last = None
                        for qg in range(16):
                            qbase = qg * 512
                            nkc = 4 * qg + 4
                            stop_tok = [None] * 4
                            st_toks = [None] * nkc

                            def emit_st(kc, qg=qg, qbase=qbase, nkc=nkc):
                                par = kc % 2
                                diag = kc >= 4 * qg
                                qoff = (kc - 4 * qg) * 128 if diag else 0
                                mov = 512 - qoff
                                kb.w("tensor", pst_free[par], future=1)
                                st_toks[kc] = kb.op("tensor", lambda e, o=P[par][:, 0:mov],
                                                    l=kbuf[bp][:, kc * 128:(kc + 1) * 128],
                                                    r=qbuf[bp][:, qbase + qoff: qbase + 512]:
                                                    e.matmul(o, l, r, start=True, stop=True),
                                                    inc="sp")

                            emit_st(0)
                            for kc in range(nkc):
                                par = kc % 2
                                diag = kc >= 4 * qg
                                qoff = (kc - 4 * qg) * 128 if diag else 0
                                mov = 512 - qoff
                                if kc + 1 < nkc:
                                    emit_st(kc + 1)
                                t_st = st_toks[kc]
                                kb.w("scalar", t_st)
                                kb.w("scalar", pt_free[par])
                                t_exp = kb.op("scalar", lambda e, o=pt[par][:, 0:mov],
                                              s=P[par][:, 0:mov]:
                                              e.activation(o, s, AF.Exp), inc="ss")
                                pst_free[par] = ("ss", t_exp[1] + 1)
                                if diag:
                                    kb.w("vector", t_exp)
                                    kb.op("vector", lambda e, o=pt[par][:, 0:128]:
                                          e.tensor_mul(o, o, tris[:, :]))
                                    t_msk = kb.echo()
                                    kb.w("tensor", t_msk)
                                else:
                                    kb.w("tensor", t_exp)
                                qb0 = (kc - 4 * qg) if diag else 0
                                for qb in range(qb0, 4):
                                    if kc == 0:
                                        kb.w("tensor", acc_free[qb])
                                    last = qb == 3
                                    t_pv = kb.op("tensor", lambda e, o=P[2 + qb][:, 0:129],
                                                 l=pt[par][:, qb * 128 - qoff: qb * 128 - qoff + 128],
                                                 r=vbuf[bp][:, kc * 129:(kc + 1) * 129],
                                                 s=(kc == 0), t=(kc == 4 * qg + qb):
                                                 e.matmul(o, l, r, start=s, stop=t),
                                                 inc="sp" if last else None)
                                for qb in range(qb0, 4):
                                    if kc == 4 * qg + qb:
                                        stop_tok[qb] = t_pv
                                pt_free[par] = t_pv
                                t_pe_last = t_pv
                            for qb in range(4):
                                po = qb % 2
                                tb_ = qb % 2
                                kb.w("vector", stop_tok[qb])
                                kb.op("vector", lambda e, s=P[2 + qb][:, 128:129], o=rcs[:, qb:qb + 1]:
                                      e.reciprocal(o, s))
                                t_rc = kb.echo()
                                kb.w("scalar", t_rc)
                                kb.w("scalar", otm_free[po])
                                t_nm = kb.op("scalar", lambda e, o=otm[po][:, :],
                                             s=P[2 + qb][:, 0:128], r=rcs[:, qb:qb + 1]:
                                             e.activation(o, s, AF.Copy, scale=r), inc="ss")
                                kb.w("tensor", t_nm)
                                kb.w("tensor", tpb_free[tb_])
                                t_tp = kb.op("tensor", lambda e, s=otm[po][:, :],
                                             o=TPBS[tb_][:, :]:
                                             e.transpose(o, s, ident[:, :]), inc="sp")
                                kb.w("scalar", t_tp)
                                kb.w("scalar", osta_free[bp] if (qg == 0 and qb == 0) else None)
                                col = qbase + qb * 128
                                t_te = kb.op("scalar", lambda e, o=osta[bp][:, col:col + 128],
                                             s=TPBS[tb_][:, :]: e.copy(o, s), inc="ss")
                                acc_free[qb] = t_te
                                tpb_free[tb_] = ("ss", t_te[1] + 1)
                                otm_free[po] = t_tp
                        pair_cons[p] = t_pe_last
                        t_o = kb.dma("sync", f"ost{bp}", cc_o_in[(hl * 2 + b) * 128:(hl * 2 + b + 1) * 128, :],
                                     osta[bp][:, :], waits=[t_te])
                        osta_free[bp] = t_o
                        if p + 2 < 4:
                            emit_pair_load(p + 2)

                    kb.w("gpsimd", kb.tot("ost0"))
                    kb.w("gpsimd", kb.tot("ost1"))
                    t_cc_o = kb.op("gpsimd", lambda e: e.collective_compute(
                        "AllGather", ALU.bypass, replica_groups=[[0, 1, 2, 3, 4, 5, 6, 7]],
                        ins=[cc_o_in], outs=[cc_o_out]), inc="scc")
                    kb.mksem("d_cpyo")
                    kb.barrier()

                if stages >= 4:
                    with ExitStack() as sd_:
                        ln_sb = sd_.enter_context(nc.sbuf_tensor("ln2_sb", [128, NDC * NT], BF16))
                        xt = [sd_.enter_context(nc.sbuf_tensor(f"xt4{i}", [128, NDC * TTS], F32))
                              for i in range(1)]
                        ofm = [sd_.enter_context(nc.sbuf_tensor(f"ofm{i}", [128, NOC * TTS], BF16))
                               for i in range(2)]
                        wsl = [sd_.enter_context(nc.sbuf_tensor(f"wsl4{i}", [128, D], BF16))
                               for i in range(2)]
                        w1sl = [sd_.enter_context(nc.sbuf_tensor(f"w1sl{i}", [128, D], BF16))
                                for i in range(2)]
                        w3sl = [sd_.enter_context(nc.sbuf_tensor(f"w3sl{i}", [128, D], BF16))
                                for i in range(2)]

                        kb.w("sync", t_cc_o)
                        oo = cc_o_out.rearrange("(r hl bb p) t -> bb r hl p t", r=8, hl=2, bb=2)
                        mo = my_o.rearrange("(r hl p) t -> r hl p t", r=8, hl=2)

                        def copy_o(e):
                            for arm in e.switch_core_id(n=128):
                                c = arm.logical % 8
                                bc, jc = c // 4, c % 4
                                e.dma_start(out=mo, in_=oo[bc, :, :, :, jc * NT:(jc + 1) * NT]
                                            ).then_inc(kb.sems["d_cpyo"], 16)
                        kb.cnt["d_cpyo"] += 16
                        kb.ops["sync"].append(copy_o)
                        t_cpyo = kb.tot("cpyo")

                        x3 = x.rearrange("(dc p) t -> p dc t", p=128)
                        mo3 = my_o.rearrange("(h p) t -> p h t", p=128)
                        ln_ready2 = [None] * NTT
                        p_free = [None] * 6
                        xt_free = [None]
                        ofm_free = [None, None]
                        wt_free = [None, None]
                        ev_free = [None, None]
                        sq_free = [None, None]
                        tmp_free = [None, None]
                        rows_free = None
                        t_x1st = None
                        for i in range(NTT):
                            ip = 0
                            op_ = i % 2
                            t_x = kb.dma("sync", "xt0", xt[ip].rearrange("p (dc t) -> p dc t", t=TTS),
                                         x3[:, :, i * TTS:(i + 1) * TTS], waits=[xt_free[ip], t_x1st])
                            t_of = kb.dma("sync", f"of{op_}", ofm[op_].rearrange("p (h t) -> p h t", t=TTS),
                                          mo3[:, :, i * TTS:(i + 1) * TTS],
                                          waits=[ofm_free[op_], t_cpyo])
                            x1_toks = []
                            for oc in range(NOC):
                                wp = oc % 2
                                t_w = kb.dma("gpsimd", f"ws{wp}", wsl[wp][:, :], wo[oc], waits=[wt_free[wp]])
                                pi = oc % 2
                                kb.w("tensor", t_w)
                                kb.w("tensor", t_of)
                                kb.w("tensor", p_free[pi])
                                for dc in range(NDC):
                                    t_p = kb.op("tensor", lambda e, o=P[pi][:, 0:512],
                                                l=wsl[wp][:, dc * 128:(dc + 1) * 128],
                                                r=ofm[op_][:, dc * TTS:(dc + 1) * TTS],
                                                s=(dc == 0), t=(dc == NDC - 1):
                                                e.matmul(o, l, r, start=s, stop=t),
                                                inc="sp" if dc == NDC - 1 else None)
                                wt_free[wp] = t_p
                                kb.w("vector", t_p)
                                kb.w("vector", t_x)
                                xs = xt[ip][:, oc * TTS:(oc + 1) * TTS]
                                kb.op("vector", lambda e, o=xs, s=P[pi][:, 0:512],
                                      bb=bos[:, oc:oc + 1]:
                                      e.scalar_tensor_tensor(o, s, bb, o, ALU.add, ALU.add))
                                sp2 = oc % 2
                                kb.w("vector", sq_free[sp2])
                                kb.op("vector", lambda e, o=sq[sp2][:, :], a=xs:
                                      e.tensor_mul(o, a, a))
                                t_x1 = kb.echo()
                                p_free[pi] = t_x1
                                x1_toks.append(t_x1)
                                t_x1st = kb.dma("sync", "x1s", x1d[oc * 128:(oc + 1) * 128, i * TTS:(i + 1) * TTS],
                                                xs, waits=[t_x1])
                                kb.w("tensor", t_x1)
                                if oc == 0:
                                    kb.w("tensor", p_free[2])
                                    kb.w("tensor", p_free[3])
                                kb.op("tensor", lambda e, r=xs, s=(oc == 0), t=(oc == NOC - 1):
                                      e.matmul(P[2][0:1, 0:512], onesc[:, 0:1], r, start=s, stop=t))
                                t_st = kb.op("tensor", lambda e, r=sq[sp2][:, :], s=(oc == 0),
                                             t=(oc == NOC - 1):
                                             e.matmul(P[3][0:1, 0:512], onesc[:, 0:1], r, start=s, stop=t),
                                             inc="sp")
                                sq_free[sp2] = t_st
                            t_stats = t_st
                            kb.w("vector", t_stats)
                            kb.w("vector", rows_free)
                            ta, tb = rows_ln(P[2], P[3], D, 0.0, True)
                            kb.w("tensor", tb)
                            kb.w("tensor", p_free[4])
                            kb.w("tensor", p_free[5])
                            kb.op("tensor", lambda e: e.matmul(P[4][:, 0:512], onesr[0:1, :],
                                                               r_ab[0:1, 0:512], start=True, stop=True))
                            t_bc = kb.op("tensor", lambda e: e.matmul(P[5][:, 0:512], onesr[0:1, :],
                                                                      r_ab[0:1, 512:1024], start=True,
                                                                      stop=True), inc="sp")
                            rows_free = t_bc
                            p_free[2] = t_bc
                            p_free[3] = t_bc
                            kb.w("vector", t_bc)
                            t_ap = None
                            for dc in range(NDC):
                                xs = xt[ip][:, dc * TTS:(dc + 1) * TTS]
                                dp = dc % 2
                                kb.w("vector", tmp_free[dp])
                                kb.op("vector", lambda e, o=tmp[dp][:, :], a=xs:
                                      e.tensor_mul(o, a, P[4][:, 0:512]))
                                dst = ln_sb[:, dc * NT + i * TTS: dc * NT + (i + 1) * TTS]
                                t_ap = kb.op("vector", lambda e, o=dst, a=tmp[dp][:, :]:
                                             e.tensor_add(o, a, P[5][:, 0:512]), inc="sv")
                                tmp_free[dp] = t_ap
                            ln_ready2[i] = kb.echo()
                            xt_free[ip] = t_ap
                            ofm_free[op_] = t_p
                            p_free[4] = ("sv", t_ap[1] + 1)
                            p_free[5] = ("sv", t_ap[1] + 1)

                        w1_free = [None, None]
                        w3_free = [None, None]
                        pt_free = [None, None]
                        for hc in range(NHC):
                            wp = hc % 2
                            t_w1 = kb.dma("gpsimd", f"w1{wp}", w1sl[wp][:, :], w1[hc], waits=[w1_free[wp]])
                            t_w3 = kb.dma("gpsimd", f"w3{wp}", w3sl[wp][:, :], w3[hc], waits=[w3_free[wp]])
                            for i in range(NTT):
                                gsel = (hc * NTT + i) % 2
                                Pg, Pl = (P[0], P[1]) if gsel == 0 else (P[2], P[3])
                                gi, li = (0, 1) if gsel == 0 else (2, 3)
                                kb.w("tensor", t_w1)
                                kb.w("tensor", t_w3)
                                kb.w("tensor", ln_ready2[i])
                                kb.w("tensor", p_free[gi])
                                kb.w("tensor", p_free[li])
                                for dc in range(NDC):
                                    kb.op("tensor", lambda e, o=Pg[:, 0:512],
                                          l=w1sl[wp][:, dc * 128:(dc + 1) * 128],
                                          r=ln_sb[:, dc * NT + i * TTS: dc * NT + (i + 1) * TTS],
                                          s=(dc == 0), t=(dc == NDC - 1):
                                          e.matmul(o, l, r, start=s, stop=t))
                                for dc in range(NDC):
                                    t_p = kb.op("tensor", lambda e, o=Pl[:, 0:512],
                                                l=w3sl[wp][:, dc * 128:(dc + 1) * 128],
                                                r=ln_sb[:, dc * NT + i * TTS: dc * NT + (i + 1) * TTS],
                                                s=(dc == 0), t=(dc == NDC - 1):
                                                e.matmul(o, l, r, start=s, stop=t),
                                                inc="sp" if dc == NDC - 1 else None)
                                if i == NTT - 1:
                                    w1_free[wp] = t_p
                                    w3_free[wp] = t_p
                                ep = i % 2
                                kb.w("scalar", t_p)
                                kb.w("scalar", ev_free[ep])
                                t_si = kb.op("scalar", lambda e, o=ev[ep][:, :], s=Pg[:, 0:512]:
                                             e.activation(o, s, AF.Silu), inc="ss")
                                p_free[gi] = ("ss", t_si[1] + 1)
                                kb.w("vector", t_si)
                                kb.w("vector", pt_free[ep])
                                t_h = kb.op("vector", lambda e, o=pt[ep][:, :], a=ev[ep][:, :],
                                            s=Pl[:, 0:512]: e.tensor_mul(o, a, s), inc="sv")
                                p_free[li] = ("sv", t_h[1] + 1)
                                ev_free[ep] = ("sv", t_h[1] + 1)
                                t_hd = kb.dma("sync", f"hd{ep}", hidd[hc * 128:(hc + 1) * 128, i * TTS:(i + 1) * TTS],
                                              pt[ep][:, :], waits=[t_h])
                                pt_free[ep] = t_hd

                        kb.barrier()

                    if stages >= 5:
                        with ExitStack() as se:
                            hblk = se.enter_context(nc.sbuf_tensor("hblk", [128, NHC * TTS], BF16))
                            w2sl = [se.enter_context(nc.sbuf_tensor(f"w2sl2{i}", [128, 64 * 128], BF16))
                                    for i in range(2)]
                            h3 = hidd.rearrange("(hc p) t -> p hc t", p=128)
                            hblk_free = None
                            w2_free = [None, None]
                            p_free = [None, None]
                            x1_free = [None, None]
                            yst_free = [None, None]
                            for i in range(NTT):
                                t_hb = kb.dma("sync", "hbl", hblk.rearrange("p (hc t) -> p hc t", t=TTS),
                                              h3[:, :, i * TTS:(i + 1) * TTS], waits=[hblk_free])
                                for oc in range(NOC):
                                    wp = oc % 2
                                    t_w = kb.dma("sync", f"wd{wp}", w2sl[wp][:, :], w2qd[oc], waits=[w2_free[wp]])
                                    xp = oc % 2
                                    t_x1 = kb.dma("sync", f"xl{xp}", x1c[xp][:, :],
                                                  x1d[oc * 128:(oc + 1) * 128, i * TTS:(i + 1) * TTS],
                                                  waits=[x1_free[xp]])
                                    pi = oc % 2
                                    kb.w("tensor", t_w)
                                    kb.w("tensor", t_hb)
                                    kb.w("tensor", p_free[pi])
                                    for a in range(NHC):
                                        t_p = kb.op("tensor", lambda e, o=P[pi][:, 0:512],
                                                    l=w2sl[wp][:, a * 128:(a + 1) * 128],
                                                    r=hblk[:, a * TTS:(a + 1) * TTS],
                                                    s=(a == 0), t=(a == NHC - 1):
                                                    e.matmul(o, l, r, start=s, stop=t),
                                                    inc="sp" if a == NHC - 1 else None)
                                    w2_free[wp] = t_p
                                    if oc == NOC - 1:
                                        hblk_free = t_p
                                    yp = oc % 2
                                    kb.w("vector", t_p)
                                    kb.w("vector", t_x1)
                                    kb.w("vector", yst_free[yp])
                                    t_y = kb.op("vector", lambda e, o=yst[yp][:, :], s=P[pi][:, 0:512],
                                                a=x1c[xp][:, :]: e.scalar_tensor_tensor(
                                                    o, s, W2SCALE, a, ALU.mult, ALU.add), inc="sv")
                                    p_free[pi] = ("sv", t_y[1] + 1)
                                    x1_free[xp] = ("sv", t_y[1] + 1)
                                    t_yo = kb.dma("sync", f"yw{yp}", y[oc * 128:(oc + 1) * 128, i * TTS:(i + 1) * TTS],
                                                  yst[yp][:, :], waits=[t_y])
                                    yst_free[yp] = t_yo

                            kb.w("sync", kb.tot("yw0"))
                            kb.w("sync", kb.tot("yw1"))

        with nc.Block() as blk:
            @blk.sync
            def _(e):
                for f in kb.ops["sync"]:
                    f(e)

            @blk.gpsimd
            def _(e):
                for f in kb.ops["gpsimd"]:
                    f(e)

            @blk.tensor
            def _(e):
                for f in kb.ops["tensor"]:
                    f(e)

            @blk.vector
            def _(e):
                for f in kb.ops["vector"]:
                    f(e)

            @blk.scalar
            def _(e):
                for f in kb.ops["scalar"]:
                    f(e)

    return nc


_NC_CACHE = None


def _get_nc():
    global _NC_CACHE
    if _NC_CACHE is None:
        _NC_CACHE = build()
    return _NC_CACHE


def prep_in_maps(L0_attn_x, L0_attn_Wq, L0_attn_Wk, L0_attn_Wv, L0_attn_Wo, L0_attn_bo,
                 L0_ffn_W1, L0_ffn_W2, L0_ffn_W3):
    xf = np.ascontiguousarray(np.asarray(L0_attn_x, np.float32))
    x2d = xf.reshape(B * L, D)

    def blockT(w, noc):
        wt = np.asarray(w, np.float32).T
        r = wt.reshape(NDC, 128, noc, 128).transpose(2, 1, 0, 3)
        return np.ascontiguousarray(r.reshape(noc, 128, NDC * 128))

    wqb = blockT(L0_attn_Wq, NOC)
    wkb = blockT(L0_attn_Wk, NOC)
    wvb = blockT(L0_attn_Wv, NOC)
    wob = blockT(L0_attn_Wo, NOC)
    w1b = blockT(L0_ffn_W1, NHC)
    w3b = blockT(L0_ffn_W3, NHC)
    w2b = np.ascontiguousarray(np.asarray(L0_ffn_W2, np.float32).T.reshape(64, 128, D))
    bob = np.ascontiguousarray(np.asarray(L0_attn_bo, np.float32).reshape(NOC, 128).T)

    in_maps = []
    for c in range(8):
        xc = np.ascontiguousarray(x2d[c * NT:(c + 1) * NT].T)
        in_maps.append({
            "x": xc, "wq": wqb, "wk": wkb, "wv": wvb, "wo": wob,
            "w1": w1b, "w3": w3b, "w2": w2b, "bo": bob,
        })
    return in_maps


def kernel(L0_attn_x, L0_attn_Wq, L0_attn_Wk, L0_attn_Wv, L0_attn_Wo, L0_attn_bo,
           L0_ffn_W1, L0_ffn_W2, L0_ffn_W3):
    in_maps = prep_in_maps(L0_attn_x, L0_attn_Wq, L0_attn_Wk, L0_attn_Wv, L0_attn_Wo,
                           L0_attn_bo, L0_ffn_W1, L0_ffn_W2, L0_ffn_W3)
    nc = _get_nc()
    res = run_bass_kernel_spmd(nc, in_maps, core_ids=list(range(8)))
    out = np.empty((B * L, D), np.float32)
    for c in range(8):
        out[c * NT:(c + 1) * NT] = np.asarray(res.results[c]["y"]).T
    return out.reshape(B, L, D)


# version 09
# speedup vs baseline: 1.0855x; 1.0855x over previous
import math
from contextlib import ExitStack

import numpy as np
import ml_dtypes

import concourse.bass as bass
import concourse.mybir as mybir
from concourse.bass_utils import run_bass_kernel_spmd

F32 = mybir.dt.float32
BF16 = mybir.dt.bfloat16
AF = mybir.ActivationFunctionType
ALU = mybir.AluOpType
bfdt = ml_dtypes.bfloat16

B, L, D, NH, HD, FF = 2, 8192, 2048, 16, 128, 8192
NT = 2048
TTS = 512
NTT = NT // TTS
NDC = D // 128
NOC = 16
NHC = FF // 128
EPS = 1e-5
LOG_RSHD = float(np.log(1.0 / np.sqrt(HD)))
W2SCALE = float(1.0 / np.sqrt(FF))
ENGS = ("sync", "gpsimd", "tensor", "vector", "scalar")


def _had(n):
    h = np.array([[1.0]], np.float32)
    while h.shape[0] < n:
        h = np.block([[h, h], [h, -h]])
    return h


class KB:

    def __init__(self, nc, st):
        self.nc = nc
        self.st = st
        self.ops = {e: [] for e in ENGS}
        self.sems = {}
        self.cnt = {}

    def mksem(self, *names):
        for n in names:
            if n not in self.sems:
                self.sems[n] = self.st.enter_context(self.nc.semaphore(n))
                self.cnt[n] = 0

    def w(self, eng, tok, future=0):
        if tok is None:
            return
        s, v = tok
        assert v <= self.cnt[s] + future, (eng, s, v, self.cnt[s])
        h = self.sems[s]
        self.ops[eng].append(lambda e, h=h, v=v: e.wait_ge(h, v))

    def op(self, eng, f, inc=None, amt=1):
        if inc is None:
            self.ops[eng].append(f)
            return None
        self.cnt[inc] += amt
        tok = (inc, self.cnt[inc])
        h = self.sems[inc]

        def g(e, f=f, h=h, a=amt):
            f(e).then_inc(h, a)
        self.ops[eng].append(g)
        return tok

    def dma(self, qeng, slot, dst, src, waits=(), **kw):
        for t in waits:
            self.w(qeng, t)
        sem = "d_" + slot
        self.mksem(sem)
        return self.op(qeng, lambda e, d=dst, s=src, kw=kw: e.dma_start(out=d, in_=s, **kw),
                       inc=sem, amt=16)

    def tot(self, slot):
        return ("d_" + slot, self.cnt["d_" + slot])

    def echo(self):
        return self.op("vector", lambda e: e.tensor_copy(
            self.escr[0:1, 0:1], self.escr[0:1, 1:2]), inc="sv")

    def barrier(self):
        base = self.cnt["sb"]
        for e in ENGS:
            self.op(e, lambda en: en.nop(), inc="sb")
        toks = [("sb", base + len(ENGS))]
        toks += [(s, self.cnt[s]) for s in self.sems if s.startswith("d_")]
        for e in ENGS:
            for t in toks:
                self.w(e, t)


def build(dbg=False, stages=5, p1a=True, p1m=3, p1post=3):
    nc = bass.Bass("TRN2", target_bir_lowering=False, debug=False)
    DBG = "ExternalOutput" if dbg else "Internal"

    x = nc.dram_tensor("x", [D, NT], F32, kind="ExternalInput").ap()
    y = nc.dram_tensor("y", [D, NT], F32, kind="ExternalOutput").ap()
    wq = nc.dram_tensor("wq", [NOC, 128, D], F32, kind="ExternalInput").ap()
    wk = nc.dram_tensor("wk", [NOC, 128, D], F32, kind="ExternalInput").ap()
    wv = nc.dram_tensor("wv", [NOC, 128, D], F32, kind="ExternalInput").ap()
    wo = nc.dram_tensor("wo", [NOC, 128, D], F32, kind="ExternalInput").ap()
    w1 = nc.dram_tensor("w1", [NHC, 128, D], F32, kind="ExternalInput").ap()
    w3 = nc.dram_tensor("w3", [NHC, 128, D], F32, kind="ExternalInput").ap()
    w2 = nc.dram_tensor("w2", [64, 128, D], F32, kind="ExternalInput").ap()
    bo = nc.dram_tensor("bo", [128, NOC], F32, kind="ExternalInput").ap()

    cc_qk_in = nc.dram_tensor("cc_qk_in", [2 * D, NT], BF16, kind="Internal").ap()
    cc_qk_out = nc.dram_tensor("cc_qk_out", [8 * 2 * D, NT], BF16, kind="Internal").ap()
    cc_v_in = nc.dram_tensor("cc_v_in", [D, 16 * 129], BF16, kind="Internal").ap()
    cc_vout = nc.dram_tensor("cc_vout", [8 * D, 16 * 129], BF16, kind="Internal").ap()
    cc_o_in = nc.dram_tensor("cc_o_in", [512, L], BF16, kind="Internal").ap()
    cc_o_out = nc.dram_tensor("cc_o_out", [8 * 512, L], BF16, kind="Internal").ap()
    my_q = nc.dram_tensor("my_q", [256, 2 * L], BF16, kind=DBG).ap()
    my_k = nc.dram_tensor("my_k", [256, 2 * L], BF16, kind=DBG).ap()
    my_v = nc.dram_tensor("my_v", [256, 2 * 64 * 129], BF16, kind=DBG).ap()
    my_o = nc.dram_tensor("my_o", [D, NT], BF16, kind=DBG).ap()
    w2m = nc.dram_tensor("w2m", [64, 128 * D], BF16, kind=DBG).ap()
    w2qd = nc.dram_tensor("w2qd", [NOC, 128, 64 * 128], BF16, kind=DBG).ap()
    x1d = nc.dram_tensor("x1d", [D, NT], F32, kind=DBG).ap()
    hidd = nc.dram_tensor("hidd", [FF, NT], BF16, kind=DBG).ap()

    c_ident = nc.inline_tensor(np.eye(128, dtype=bfdt), name="c_ident").ap()
    c_h64 = nc.inline_tensor(_had(64).astype(bfdt), name="c_h64").ap()
    c_h128 = nc.inline_tensor(_had(128).astype(bfdt), name="c_h128").ap()
    tri = (np.arange(128)[:, None] <= np.arange(128)[None, :]).astype(bfdt)
    c_tri = nc.inline_tensor(tri, name="c_tri").ap()
    c_onesc = nc.inline_tensor(np.ones((128, 1), np.float32), name="c_onesc").ap()
    c_onesr = nc.inline_tensor(np.ones((1, 128), np.float32), name="c_onesr").ap()

    with ExitStack() as st:
        E = st.enter_context
        kb = KB(nc, st)
        kb.mksem("scc", "sp", "sv", "ss", "sb")

        ident = E(nc.sbuf_tensor("ident", [128, 128], BF16))
        h64s = E(nc.sbuf_tensor("h64s", [64, 64], BF16))
        h128s = E(nc.sbuf_tensor("h128s", [128, 128], BF16))
        tris = E(nc.sbuf_tensor("tris", [128, 128], BF16))
        onesc = E(nc.sbuf_tensor("onesc", [128, 1], F32))
        onesr = E(nc.sbuf_tensor("onesr", [1, 128], F32))
        bos = E(nc.sbuf_tensor("bos", [128, NOC], F32))
        r_mu = E(nc.sbuf_tensor("r_mu", [1, 512], F32))
        r_mu2 = E(nc.sbuf_tensor("r_mu2", [1, 512], F32))
        r_var = E(nc.sbuf_tensor("r_var", [1, 512], F32))
        r_lnv = E(nc.sbuf_tensor("r_lnv", [1, 512], F32))
        r_ab = E(nc.sbuf_tensor("r_ab", [1, 1024], F32))
        ev = [E(nc.sbuf_tensor(f"ev{i}", [128, 512], F32)) for i in range(2)]
        sq = [E(nc.sbuf_tensor(f"sq{i}", [128, 512], F32)) for i in range(2)]
        tmp = [E(nc.sbuf_tensor(f"tmp{i}", [128, 512], F32)) for i in range(2)]
        qkst = [E(nc.sbuf_tensor(f"qkst{i}", [128, NT], BF16)) for i in range(2)]
        vst = [E(nc.sbuf_tensor(f"vst{i}", [128, 16 * 129], BF16)) for i in range(2)]
        pt = [E(nc.sbuf_tensor(f"pt{i}", [128, 512], BF16)) for i in range(2)]
        otm = [E(nc.sbuf_tensor(f"otm{i}", [128, 128], BF16)) for i in range(2)]
        rcs = E(nc.sbuf_tensor("rcs", [128, 4], F32))
        kb_escr = E(nc.sbuf_tensor("kb_escr", [1, 2], F32))
        x1c = [E(nc.sbuf_tensor(f"x1c{i}", [128, 512], F32)) for i in range(2)]
        yst = [E(nc.sbuf_tensor(f"yst{i}", [128, 512], F32)) for i in range(2)]

        P = [E(nc.psum_tensor(f"P{i}", [128, 512], F32)) for i in range(6)]
        TPBS = [E(nc.psum_tensor(f"TPB{i}", [128, 128], BF16)) for i in range(2)]

        kb.escr = kb_escr
        for dst, src in ((ident, c_ident), (h64s, c_h64), (h128s, c_h128), (tris, c_tri),
                         (onesc, c_onesc), (onesr, c_onesr), (bos, bo)):
            kb.dma("sync", "cst", dst[:, :], src)
        t_init = kb.tot("cst")
        kb.op("vector", lambda e: e.memset(kb_escr[0:1, :], 0.0))
        for i in range(2):
            tok = kb.op("vector", lambda e, b=vst[i]: e.memset(b[:, :], 1.0), inc="sv")
        t_vst_ones = tok
        for e in ENGS:
            kb.w(e, t_init)
        kb.w("scalar", t_vst_ones)
        kb.w("sync", t_vst_ones)

        def rows_ln(stat_a, stat_b, nfeat, exp_bias, want_b):
            inv = 1.0 / nfeat
            kb.op("vector", lambda e: e.tensor_scalar_mul(r_mu[0:1, :], stat_a[0:1, 0:512], inv))
            kb.op("vector", lambda e: e.tensor_mul(r_mu2[0:1, :], r_mu[0:1, :], r_mu[0:1, :]))
            kb.op("vector", lambda e: e.scalar_tensor_tensor(
                r_var[0:1, :], stat_b[0:1, 0:512], inv, r_mu2[0:1, :], ALU.mult, ALU.subtract))
            kb.op("vector", lambda e: e.tensor_scalar_add(
                r_var[0:1, :], r_var[0:1, :], EPS))
            tv = kb.echo()
            kb.w("scalar", tv)
            tl = kb.op("scalar", lambda e: e.activation(r_lnv[0:1, :], r_var[0:1, :], AF.Ln),
                       inc="ss")
            kb.w("vector", tl)
            if exp_bias != 0.0:
                kb.op("vector", lambda e, bb=exp_bias: e.tensor_scalar(
                    r_lnv[0:1, :], r_lnv[0:1, :], -0.5, bb, ALU.mult, ALU.add))
            else:
                kb.op("vector", lambda e: e.tensor_scalar_mul(
                    r_lnv[0:1, :], r_lnv[0:1, :], -0.5))
            tm = kb.echo()
            kb.w("scalar", tm)
            ta = kb.op("scalar", lambda e: e.activation(
                r_ab[0:1, 0:512], r_lnv[0:1, :], AF.Exp), inc="ss")
            if want_b:
                kb.w("vector", ta)
                kb.op("vector", lambda e: e.tensor_mul(r_ab[0:1, 512:1024], r_mu[0:1, :],
                                                       r_ab[0:1, 0:512]))
                kb.op("vector", lambda e: e.tensor_scalar_mul(
                    r_ab[0:1, 512:1024], r_ab[0:1, 512:1024], -1.0))
                tb = kb.echo()
                return ta, tb
            return ta, None

        with ExitStack() as sa:
            ln_sb = sa.enter_context(nc.sbuf_tensor("ln_sb", [128, NDC * NT], BF16))
            xt = [sa.enter_context(nc.sbuf_tensor(f"xt{i}", [128, NDC * TTS], F32))
                  for i in range(2)]
            wsl = [sa.enter_context(nc.sbuf_tensor(f"wsl{i}", [128, D], BF16))
                   for i in range(2)]

            x3 = x.rearrange("(dc p) t -> p dc t", p=128)
            ln_ready = [None] * NTT
            xt_free = [None, None]
            rows_fr = [None]
            p_free = [None] * 6
            tpb_free = None
            ev_free = [None, None]
            sq_free = [None, None]
            pt_free = [None, None]
            tmp_free = [None, None]

            for i in range(NTT if p1a else 0):
                ip = i % 2
                t_x = kb.dma("sync", f"xt{ip}", xt[ip].rearrange("p (dc t) -> p dc t", t=TTS),
                             x3[:, :, i * TTS:(i + 1) * TTS], waits=[xt_free[ip]])
                kb.w("tensor", t_x)
                kb.w("tensor", p_free[0])
                kb.w("tensor", p_free[1])
                kb.w("vector", t_x)
                sq_toks = []
                for dc in range(NDC):
                    xs = xt[ip][:, dc * TTS:(dc + 1) * TTS]
                    kb.op("tensor", lambda e, o=P[0][0:1, 0:512], l=onesc[:, 0:1], r=xs,
                          s=(dc == 0), t=(dc == NDC - 1): e.matmul(o, l, r, start=s, stop=t))
                    dp = dc % 2
                    if dc >= 2:
                        kb.w("vector", sq_toks[dc - 2][1])
                    kb.op("vector", lambda e, o=sq[dp][:, :], a=xs: e.tensor_mul(o, a, a))
                    tsq = kb.echo()
                    kb.w("tensor", tsq)
                    tmm = kb.op("tensor", lambda e, o=P[1][0:1, 0:512], l=onesc[:, 0:1],
                                r=sq[dp][:, :], s=(dc == 0), t=(dc == NDC - 1):
                                e.matmul(o, l, r, start=s, stop=t), inc="sp")
                    sq_toks.append((tsq, tmm))
                t_stats = sq_toks[-1][1]
                kb.w("vector", t_stats)
                kb.w("vector", rows_fr[0])
                ta, tb = rows_ln(P[0], P[1], D, 0.0, True)
                kb.w("tensor", tb)
                kb.w("tensor", p_free[2])
                kb.w("tensor", p_free[3])
                kb.op("tensor", lambda e: e.matmul(P[2][:, 0:512], onesr[0:1, :],
                                                   r_ab[0:1, 0:512], start=True, stop=True))
                t_bc = kb.op("tensor", lambda e: e.matmul(P[3][:, 0:512], onesr[0:1, :],
                                                          r_ab[0:1, 512:1024], start=True,
                                                          stop=True), inc="sp")
                rows_fr[0] = t_bc
                p_free[0] = t_bc
                p_free[1] = t_bc
                kb.w("vector", t_bc)
                t_ap = None
                for dc in range(NDC):
                    xs = xt[ip][:, dc * TTS:(dc + 1) * TTS]
                    dp = dc % 2
                    if dc >= 2:
                        kb.w("vector", tmp_free[dp])
                    kb.op("vector", lambda e, o=tmp[dp][:, :], a=xs: e.tensor_mul(o, a, P[2][:, 0:512]))
                    dst = ln_sb[:, dc * NT + i * TTS: dc * NT + (i + 1) * TTS]
                    t_ap = kb.op("vector", lambda e, o=dst, a=tmp[dp][:, :]:
                                 e.tensor_add(o, a, P[3][:, 0:512]), inc="sv")
                    tmp_free[dp] = t_ap
                ln_ready[i] = kb.echo()
                xt_free[ip] = t_ap
                p_free[2] = ("sv", t_ap[1] + 1)
                p_free[3] = ("sv", t_ap[1] + 1)

            wt_free = [None, None]
            qkst_free = [None, None]
            vst_free = [None, None]
            tpb_free = [None, None]
            gtile = 0

            def post_1b(ctx):
                m, oc, i, ip, pi, t_p, op = ctx
                PP = P[pi]
                if m < 2:
                    kb.w("scalar", t_p)
                    kb.w("scalar", ev_free[ip])
                    t_ev = kb.op("scalar", lambda e, o=ev[ip][:, :], s=PP[:, 0:512]:
                                 e.copy(o, s), inc="ss")
                    p_free[pi] = ("ss", t_ev[1] + 1)
                    kb.w("vector", t_ev)
                    kb.w("vector", sq_free[ip])
                    kb.op("vector", lambda e, o=sq[ip][:, :], a=ev[ip][:, :]:
                          e.tensor_mul(o, a, a))
                    t_sq = kb.echo()
                    kb.w("tensor", t_ev)
                    kb.w("tensor", p_free[2])
                    kb.w("tensor", p_free[3])
                    kb.op("tensor", lambda e, r=ev[ip][:, :]: e.matmul(
                        P[2][0:1, 0:512], onesc[:, 0:1], r, start=True, stop=True))
                    kb.w("tensor", t_sq)
                    t_st = kb.op("tensor", lambda e, r=sq[ip][:, :]: e.matmul(
                        P[3][0:1, 0:512], onesc[:, 0:1], r, start=True, stop=True),
                        inc="sp")
                    sq_free[ip] = t_st
                    kb.w("vector", t_st)
                    kb.w("vector", rows_fr[0])
                    ta, tb = rows_ln(P[2], P[3], HD, LOG_RSHD if m == 0 else 0.0,
                                     m == 1)
                    kb.w("tensor", tb if m == 1 else ta)
                    kb.w("tensor", p_free[4])
                    kb.w("tensor", p_free[5] if m == 1 else None)
                    if m == 1:
                        kb.op("tensor", lambda e: e.matmul(
                            P[4][:, 0:512], onesr[0:1, :], r_ab[0:1, 0:512],
                            start=True, stop=True))
                        t_bc = kb.op("tensor", lambda e: e.matmul(
                            P[5][:, 0:512], onesr[0:1, :], r_ab[0:1, 512:1024],
                            start=True, stop=True), inc="sp")
                    else:
                        t_bc = kb.op("tensor", lambda e: e.matmul(
                            P[4][:, 0:512], onesr[0:1, :], r_ab[0:1, 0:512],
                            start=True, stop=True), inc="sp")
                    rows_fr[0] = t_bc
                    p_free[2] = t_bc
                    p_free[3] = t_bc
                    kb.w("vector", t_bc)
                    kb.w("vector", qkst_free[op])
                    dst = qkst[op][:, i * TTS:(i + 1) * TTS]
                    if m == 0:
                        t_sg = kb.op("vector", lambda e, o=dst, a=ev[ip][:, :]:
                                     e.tensor_mul(o, a, P[4][:, 0:512]), inc="sv")
                    else:
                        kb.w("vector", tmp_free[0])
                        kb.op("vector", lambda e, a=ev[ip][:, :]:
                              e.tensor_mul(tmp[0][:, :], a, P[4][:, 0:512]))
                        t_sg = kb.op("vector", lambda e, o=dst:
                                     e.tensor_add(o, tmp[0][:, :], P[5][:, 0:512]),
                                     inc="sv")
                        tmp_free[0] = t_sg
                    ev_free[ip] = ("sv", t_sg[1] + 1)
                    p_free[4] = ("sv", t_sg[1] + 1)
                    p_free[5] = ("sv", t_sg[1] + 1)
                    t_last = t_sg
                else:
                    kb.w("scalar", t_p)
                    kb.w("scalar", pt_free[ip])
                    t_vb = kb.op("scalar", lambda e, o=pt[ip][:, :], s=PP[:, 0:512]:
                                 e.copy(o, s), inc="ss")
                    p_free[pi] = ("ss", t_vb[1] + 1)
                    for qb in range(4):
                        tb_ = qb % 2
                        kb.w("tensor", t_vb)
                        kb.w("tensor", tpb_free[tb_], future=1)
                        t_tp = kb.op("tensor", lambda e, s=pt[ip][:, qb * 128:(qb + 1) * 128],
                                     o=TPBS[tb_][:, :]:
                                     e.transpose(o, s, ident[:, :]), inc="sp")
                        if qb == 3:
                            pt_free[ip] = t_tp
                        kb.w("scalar", t_tp)
                        kb.w("scalar", vst_free[op])
                        col = (i * 4 + qb) * 129
                        t_te = kb.op("scalar", lambda e, o=vst[op][:, col:col + 128],
                                     s=TPBS[tb_][:, :]: e.copy(o, s), inc="ss")
                        tpb_free[tb_] = ("ss", t_te[1] + 1)
                        t_last = t_te
                if i == NTT - 1:
                    if m == 0:
                        dst = cc_qk_in[oc * 128:(oc + 1) * 128, :]
                        t_out = kb.dma("sync", f"qs{op}", dst, qkst[op][:, :], waits=[t_last])
                        qkst_free[op] = t_out
                    elif m == 1:
                        dst = cc_qk_in[D + oc * 128: D + (oc + 1) * 128, :]
                        t_out = kb.dma("sync", f"qs{op}", dst, qkst[op][:, :], waits=[t_last])
                        qkst_free[op] = t_out
                    else:
                        dst = cc_v_in[oc * 128:(oc + 1) * 128, :]
                        t_out = kb.dma("sync", f"vs{op}", dst, vst[op][:, :], waits=[t_last])
                        vst_free[op] = t_out

            pending = None
            for m, wten in enumerate((wq, wk, wv)[:p1m]):
                for oc in range(NOC):
                    wp = oc % 2
                    t_w = kb.dma("gpsimd", f"ws{wp}", wsl[wp][:, :], wten[oc], waits=[wt_free[wp]])
                    op = oc % 2
                    for i in range(NTT):
                        ip = i % 2
                        pi = gtile % 2
                        gtile += 1
                        PP = P[pi]
                        kb.w("tensor", t_w)
                        kb.w("tensor", ln_ready[i])
                        kb.w("tensor", p_free[pi], future=2)
                        for dc in range(NDC):
                            t_p = kb.op("tensor", lambda e, o=PP[:, 0:512],
                                        l=wsl[wp][:, dc * 128:(dc + 1) * 128],
                                        r=ln_sb[:, dc * NT + i * TTS: dc * NT + (i + 1) * TTS],
                                        s=(dc == 0), t=(dc == NDC - 1):
                                        e.matmul(o, l, r, start=s, stop=t),
                                        inc="sp" if dc == NDC - 1 else None)
                        wt_free[wp] = t_p
                        cur = (m, oc, i, ip, pi, t_p, op)
                        if pending is not None:
                            post_1b(pending)
                        pending = cur
            post_1b(pending)

            kb.mksem("d_qs0", "d_qs1", "d_vs0", "d_vs1")
            t_p1_dma = [kb.tot("qs0"), kb.tot("qs1"), kb.tot("vs0"), kb.tot("vs1")]

            kb.barrier()

        if stages >= 2:
            for t in t_p1_dma:
                kb.w("gpsimd", t)
            t_cc_qk = kb.op("gpsimd", lambda e: e.collective_compute(
                "AllGather", ALU.bypass, replica_groups=[[0, 1, 2, 3, 4, 5, 6, 7]],
                ins=[cc_qk_in], outs=[cc_qk_out]), inc="scc")
            t_cc_v = kb.op("gpsimd", lambda e: e.collective_compute(
                "AllGather", ALU.bypass, replica_groups=[[0, 1, 2, 3, 4, 5, 6, 7]],
                ins=[cc_v_in], outs=[cc_vout]), inc="scc")

            with ExitStack() as sb:
                w2sl = [sb.enter_context(nc.sbuf_tensor(f"w2sl{i}", [64, 8192], BF16))
                        for i in range(2)]
                f1st = sb.enter_context(nc.sbuf_tensor("f1st", [64, 8192], BF16))
                w2f = [sb.enter_context(nc.sbuf_tensor(f"w2f{i}", [128, D], BF16))
                       for i in range(2)]
                f2st = sb.enter_context(nc.sbuf_tensor("f2st", [128, D], BF16))
                w2flat = w2.rearrange("a b d -> a (b d)")

                sl_free = [None, None]
                p_free = [None, None]
                f1_free = None
                for s in range(32):
                    sp_ = s % 2
                    t_w = kb.dma("gpsimd", f"fw{sp_}", w2sl[sp_][:, :],
                                 w2flat[:, s * 8192:(s + 1) * 8192], waits=[sl_free[sp_]])
                    seg_toks = []
                    for n in range(16):
                        pi = n % 2
                        kb.w("tensor", t_w)
                        kb.w("tensor", p_free[pi])
                        t_mm = kb.op("tensor", lambda e, o=P[pi][0:64, 0:512],
                                     r=w2sl[sp_][:, n * 512:(n + 1) * 512]:
                                     e.matmul(o, h64s[0:64, 0:64], r, start=True, stop=True),
                                     inc="sp")
                        kb.w("scalar", t_mm)
                        kb.w("scalar", f1_free if n == 0 else None)
                        t_ev = kb.op("scalar", lambda e, o=f1st[:, n * 512:(n + 1) * 512],
                                     s2=P[pi][0:64, 0:512]:
                                     e.copy(o, s2), inc="ss")
                        p_free[pi] = t_ev
                        seg_toks.append(t_ev)
                    sl_free[sp_] = seg_toks[-1]
                    f1_free = kb.dma("sync", "f1w", w2m[:, s * 8192:(s + 1) * 8192], f1st[:, :],
                                     waits=[seg_toks[-1]])
                t_w2m = kb.tot("f1w")
                f2_free = None
                wf_free = [None, None]
                for a in range(64):
                    ap_ = a % 2
                    t_w = kb.dma("sync", f"fl{ap_}", w2f[ap_][:, :],
                                 w2m[a:a + 1, :].rearrange("o (b d) -> (o b) d", b=128),
                                 waits=[wf_free[ap_], t_w2m if a == 0 else None])
                    seg_toks = []
                    for n in range(4):
                        pi = n % 2
                        kb.w("tensor", t_w)
                        kb.w("tensor", p_free[pi])
                        t_mm = kb.op("tensor", lambda e, o=P[pi][:, 0:512],
                                     r=w2f[ap_][:, n * 512:(n + 1) * 512]:
                                     e.matmul(o, h128s[:, :], r, start=True, stop=True), inc="sp")
                        kb.w("scalar", t_mm)
                        kb.w("scalar", f2_free if n == 0 else None)
                        t_ev = kb.op("scalar", lambda e, o=f2st[:, n * 512:(n + 1) * 512],
                                     s2=P[pi][:, 0:512]: e.copy(o, s2), inc="ss")
                        p_free[pi] = ("ss", t_ev[1] + 1)
                        seg_toks.append(t_ev)
                    wf_free[ap_] = seg_toks[-1]
                    f2_free = kb.dma(
                        "sync", "f2w",
                        w2qd.rearrange("oc b f -> b oc f")[:, :, a * 128:(a + 1) * 128],
                        f2st.rearrange("b (oc o) -> b oc o", oc=16),
                        waits=[seg_toks[-1]])
                t_w2q = kb.tot("f2w")

                kb.w("sync", t_cc_qk)
                kb.w("sync", t_cc_v)
                qko = cc_qk_out.rearrange("(r hr) t -> r hr t", hr=2 * D)
                vo = cc_vout.rearrange("(r hr) f -> r hr f", hr=D)
                mq = my_q.rearrange("(hl p) (b j t) -> hl b j p t", hl=2, b=2, j=4)
                mk = my_k.rearrange("(hl p) (b j t) -> hl b j p t", hl=2, b=2, j=4)
                mv = my_v.rearrange("(hl p) (b j f) -> hl b j p f", hl=2, b=2, j=4)

                kb.mksem("d_cpy")

                def copy_block(e):
                    for arm in e.switch_core_id(n=128):
                        c = arm.logical % 8
                        for hl in range(2):
                            h = 2 * c + hl
                            for b in range(2):
                                e.dma_start(out=mq[hl, b], in_=qko[4 * b:4 * b + 4,
                                            h * 128:(h + 1) * 128, :]).then_inc(kb.sems["d_cpy"], 16)
                                e.dma_start(out=mk[hl, b], in_=qko[4 * b:4 * b + 4,
                                            D + h * 128: D + (h + 1) * 128, :]).then_inc(kb.sems["d_cpy"], 16)
                                e.dma_start(out=mv[hl, b], in_=vo[4 * b:4 * b + 4,
                                            h * 128:(h + 1) * 128, :]).then_inc(kb.sems["d_cpy"], 16)
                kb.cnt["d_cpy"] += 12 * 16
                kb.ops["sync"].append(copy_block)
                t_copy = kb.tot("cpy")

                kb.barrier()

            if stages >= 3:
                with ExitStack() as sc:
                    kbuf = [sc.enter_context(nc.sbuf_tensor(f"kbuf{i}", [128, L], BF16))
                            for i in range(2)]
                    vbuf = [sc.enter_context(nc.sbuf_tensor(f"vbuf{i}", [128, 64 * 129], BF16))
                            for i in range(2)]
                    qbuf = [sc.enter_context(nc.sbuf_tensor(f"qbuf{i}", [128, L], BF16))
                            for i in range(2)]
                    osta = [sc.enter_context(nc.sbuf_tensor(f"osta{i}", [128, L], BF16))
                            for i in range(2)]

                    pair_cons = [None] * 4
                    osta_free = [None, None]
                    ld_toks = [None] * 4

                    def emit_pair_load(p):
                        hl, b = p // 2, p % 2
                        bp = p % 2
                        w_ = [pair_cons[p - 2] if p >= 2 else None]
                        t1 = kb.dma("sync", f"kb{bp}", kbuf[bp][:, :],
                                    my_k[hl * 128:(hl + 1) * 128, b * L:(b + 1) * L], waits=w_)
                        t2 = kb.dma("sync", f"vb{bp}", vbuf[bp][:, :],
                                    my_v[hl * 128:(hl + 1) * 128, b * 8256:(b + 1) * 8256])
                        t3 = kb.dma("sync", f"qb{bp}", qbuf[bp][:, :],
                                    my_q[hl * 128:(hl + 1) * 128, b * L:(b + 1) * L])
                        ld_toks[p] = (t1, t2, t3)

                    kb.w("sync", t_copy)
                    emit_pair_load(0)
                    emit_pair_load(1)

                    pst_free = [None, None]
                    pt_free = [None, None]
                    acc_free = [None] * 4
                    tpb_free = [None, None]
                    otm_free = [None, None]

                    for p in range(4):
                        hl, b = p // 2, p % 2
                        bp = p % 2
                        for t in ld_toks[p]:
                            kb.w("tensor", t)
                        t_pe_last = None
                        for qg in range(16):
                            qbase = qg * 512
                            nkc = 4 * qg + 4
                            stop_tok = [None] * 4
                            st_toks = [None] * nkc

                            def emit_st(kc, qg=qg, qbase=qbase, nkc=nkc):
                                par = kc % 2
                                diag = kc >= 4 * qg
                                qoff = (kc - 4 * qg) * 128 if diag else 0
                                mov = 512 - qoff
                                kb.w("tensor", pst_free[par], future=1)
                                st_toks[kc] = kb.op("tensor", lambda e, o=P[par][:, 0:mov],
                                                    l=kbuf[bp][:, kc * 128:(kc + 1) * 128],
                                                    r=qbuf[bp][:, qbase + qoff: qbase + 512]:
                                                    e.matmul(o, l, r, start=True, stop=True),
                                                    inc="sp")

                            emit_st(0)
                            for kc in range(nkc):
                                par = kc % 2
                                diag = kc >= 4 * qg
                                qoff = (kc - 4 * qg) * 128 if diag else 0
                                mov = 512 - qoff
                                if kc + 1 < nkc:
                                    emit_st(kc + 1)
                                t_st = st_toks[kc]
                                kb.w("scalar", t_st)
                                kb.w("scalar", pt_free[par])
                                t_exp = kb.op("scalar", lambda e, o=pt[par][:, 0:mov],
                                              s=P[par][:, 0:mov]:
                                              e.activation(o, s, AF.Exp), inc="ss")
                                pst_free[par] = ("ss", t_exp[1] + 1)
                                if diag:
                                    kb.w("vector", t_exp)
                                    kb.op("vector", lambda e, o=pt[par][:, 0:128]:
                                          e.tensor_mul(o, o, tris[:, :]))
                                    t_msk = kb.echo()
                                    kb.w("tensor", t_msk)
                                else:
                                    kb.w("tensor", t_exp)
                                qb0 = (kc - 4 * qg) if diag else 0
                                for qb in range(qb0, 4):
                                    if kc == 0:
                                        kb.w("tensor", acc_free[qb])
                                    last = qb == 3
                                    t_pv = kb.op("tensor", lambda e, o=P[2 + qb][:, 0:129],
                                                 l=pt[par][:, qb * 128 - qoff: qb * 128 - qoff + 128],
                                                 r=vbuf[bp][:, kc * 129:(kc + 1) * 129],
                                                 s=(kc == 0), t=(kc == 4 * qg + qb):
                                                 e.matmul(o, l, r, start=s, stop=t),
                                                 inc="sp" if last else None)
                                for qb in range(qb0, 4):
                                    if kc == 4 * qg + qb:
                                        stop_tok[qb] = t_pv
                                pt_free[par] = t_pv
                                t_pe_last = t_pv
                            for qb in range(4):
                                po = qb % 2
                                tb_ = qb % 2
                                kb.w("vector", stop_tok[qb])
                                kb.op("vector", lambda e, s=P[2 + qb][:, 128:129], o=rcs[:, qb:qb + 1]:
                                      e.reciprocal(o, s))
                                t_rc = kb.echo()
                                kb.w("scalar", t_rc)
                                kb.w("scalar", otm_free[po])
                                t_nm = kb.op("scalar", lambda e, o=otm[po][:, :],
                                             s=P[2 + qb][:, 0:128], r=rcs[:, qb:qb + 1]:
                                             e.activation(o, s, AF.Copy, scale=r), inc="ss")
                                kb.w("tensor", t_nm)
                                kb.w("tensor", tpb_free[tb_])
                                t_tp = kb.op("tensor", lambda e, s=otm[po][:, :],
                                             o=TPBS[tb_][:, :]:
                                             e.transpose(o, s, ident[:, :]), inc="sp")
                                kb.w("scalar", t_tp)
                                kb.w("scalar", osta_free[bp] if (qg == 0 and qb == 0) else None)
                                col = qbase + qb * 128
                                t_te = kb.op("scalar", lambda e, o=osta[bp][:, col:col + 128],
                                             s=TPBS[tb_][:, :]: e.copy(o, s), inc="ss")
                                acc_free[qb] = t_te
                                tpb_free[tb_] = ("ss", t_te[1] + 1)
                                otm_free[po] = t_tp
                        pair_cons[p] = t_pe_last
                        t_o = kb.dma("sync", f"ost{bp}", cc_o_in[(hl * 2 + b) * 128:(hl * 2 + b + 1) * 128, :],
                                     osta[bp][:, :], waits=[t_te])
                        osta_free[bp] = t_o
                        if p + 2 < 4:
                            emit_pair_load(p + 2)

                    kb.w("gpsimd", kb.tot("ost0"))
                    kb.w("gpsimd", kb.tot("ost1"))
                    t_cc_o = kb.op("gpsimd", lambda e: e.collective_compute(
                        "AllGather", ALU.bypass, replica_groups=[[0, 1, 2, 3, 4, 5, 6, 7]],
                        ins=[cc_o_in], outs=[cc_o_out]), inc="scc")
                    kb.mksem("d_cpyo")
                    kb.barrier()

                if stages >= 4:
                    with ExitStack() as sd_:
                        ln_sb = sd_.enter_context(nc.sbuf_tensor("ln2_sb", [128, NDC * NT], BF16))
                        xt = [sd_.enter_context(nc.sbuf_tensor(f"xt4{i}", [128, NDC * TTS], F32))
                              for i in range(1)]
                        ofm = [sd_.enter_context(nc.sbuf_tensor(f"ofm{i}", [128, NOC * TTS], BF16))
                               for i in range(2)]
                        wsl = [sd_.enter_context(nc.sbuf_tensor(f"wsl4{i}", [128, D], BF16))
                               for i in range(2)]
                        w1sl = [sd_.enter_context(nc.sbuf_tensor(f"w1sl{i}", [128, D], BF16))
                                for i in range(2)]
                        w3sl = [sd_.enter_context(nc.sbuf_tensor(f"w3sl{i}", [128, D], BF16))
                                for i in range(2)]

                        kb.w("sync", t_cc_o)
                        oo = cc_o_out.rearrange("(r hl bb p) t -> bb r hl p t", r=8, hl=2, bb=2)
                        mo = my_o.rearrange("(r hl p) t -> r hl p t", r=8, hl=2)

                        def copy_o(e):
                            for arm in e.switch_core_id(n=128):
                                c = arm.logical % 8
                                bc, jc = c // 4, c % 4
                                e.dma_start(out=mo, in_=oo[bc, :, :, :, jc * NT:(jc + 1) * NT]
                                            ).then_inc(kb.sems["d_cpyo"], 16)
                        kb.cnt["d_cpyo"] += 16
                        kb.ops["sync"].append(copy_o)
                        t_cpyo = kb.tot("cpyo")

                        x3 = x.rearrange("(dc p) t -> p dc t", p=128)
                        mo3 = my_o.rearrange("(h p) t -> p h t", p=128)
                        ln_ready2 = [None] * NTT
                        p_free = [None] * 6
                        xt_free = [None]
                        ofm_free = [None, None]
                        wt_free = [None, None]
                        ev_free = [None, None]
                        sq_free = [None, None]
                        tmp_free = [None, None]
                        rows_free = None
                        t_x1st = None
                        for i in range(NTT):
                            ip = 0
                            op_ = i % 2
                            t_x = kb.dma("sync", "xt0", xt[ip].rearrange("p (dc t) -> p dc t", t=TTS),
                                         x3[:, :, i * TTS:(i + 1) * TTS], waits=[xt_free[ip], t_x1st])
                            t_of = kb.dma("sync", f"of{op_}", ofm[op_].rearrange("p (h t) -> p h t", t=TTS),
                                          mo3[:, :, i * TTS:(i + 1) * TTS],
                                          waits=[ofm_free[op_], t_cpyo])
                            x1_toks = []
                            for oc in range(NOC):
                                wp = oc % 2
                                t_w = kb.dma("gpsimd", f"ws{wp}", wsl[wp][:, :], wo[oc], waits=[wt_free[wp]])
                                pi = oc % 2
                                kb.w("tensor", t_w)
                                kb.w("tensor", t_of)
                                kb.w("tensor", p_free[pi])
                                for dc in range(NDC):
                                    t_p = kb.op("tensor", lambda e, o=P[pi][:, 0:512],
                                                l=wsl[wp][:, dc * 128:(dc + 1) * 128],
                                                r=ofm[op_][:, dc * TTS:(dc + 1) * TTS],
                                                s=(dc == 0), t=(dc == NDC - 1):
                                                e.matmul(o, l, r, start=s, stop=t),
                                                inc="sp" if dc == NDC - 1 else None)
                                wt_free[wp] = t_p
                                kb.w("vector", t_p)
                                kb.w("vector", t_x)
                                xs = xt[ip][:, oc * TTS:(oc + 1) * TTS]
                                kb.op("vector", lambda e, o=xs, s=P[pi][:, 0:512],
                                      bb=bos[:, oc:oc + 1]:
                                      e.scalar_tensor_tensor(o, s, bb, o, ALU.add, ALU.add))
                                sp2 = oc % 2
                                kb.w("vector", sq_free[sp2])
                                kb.op("vector", lambda e, o=sq[sp2][:, :], a=xs:
                                      e.tensor_mul(o, a, a))
                                t_x1 = kb.echo()
                                p_free[pi] = t_x1
                                x1_toks.append(t_x1)
                                t_x1st = kb.dma("sync", "x1s", x1d[oc * 128:(oc + 1) * 128, i * TTS:(i + 1) * TTS],
                                                xs, waits=[t_x1])
                                kb.w("tensor", t_x1)
                                if oc == 0:
                                    kb.w("tensor", p_free[2])
                                    kb.w("tensor", p_free[3])
                                kb.op("tensor", lambda e, r=xs, s=(oc == 0), t=(oc == NOC - 1):
                                      e.matmul(P[2][0:1, 0:512], onesc[:, 0:1], r, start=s, stop=t))
                                t_st = kb.op("tensor", lambda e, r=sq[sp2][:, :], s=(oc == 0),
                                             t=(oc == NOC - 1):
                                             e.matmul(P[3][0:1, 0:512], onesc[:, 0:1], r, start=s, stop=t),
                                             inc="sp")
                                sq_free[sp2] = t_st
                            t_stats = t_st
                            kb.w("vector", t_stats)
                            kb.w("vector", rows_free)
                            ta, tb = rows_ln(P[2], P[3], D, 0.0, True)
                            kb.w("tensor", tb)
                            kb.w("tensor", p_free[4])
                            kb.w("tensor", p_free[5])
                            kb.op("tensor", lambda e: e.matmul(P[4][:, 0:512], onesr[0:1, :],
                                                               r_ab[0:1, 0:512], start=True, stop=True))
                            t_bc = kb.op("tensor", lambda e: e.matmul(P[5][:, 0:512], onesr[0:1, :],
                                                                      r_ab[0:1, 512:1024], start=True,
                                                                      stop=True), inc="sp")
                            rows_free = t_bc
                            p_free[2] = t_bc
                            p_free[3] = t_bc
                            kb.w("vector", t_bc)
                            t_ap = None
                            for dc in range(NDC):
                                xs = xt[ip][:, dc * TTS:(dc + 1) * TTS]
                                dp = dc % 2
                                kb.w("vector", tmp_free[dp])
                                kb.op("vector", lambda e, o=tmp[dp][:, :], a=xs:
                                      e.tensor_mul(o, a, P[4][:, 0:512]))
                                dst = ln_sb[:, dc * NT + i * TTS: dc * NT + (i + 1) * TTS]
                                t_ap = kb.op("vector", lambda e, o=dst, a=tmp[dp][:, :]:
                                             e.tensor_add(o, a, P[5][:, 0:512]), inc="sv")
                                tmp_free[dp] = t_ap
                            ln_ready2[i] = kb.echo()
                            xt_free[ip] = t_ap
                            ofm_free[op_] = t_p
                            p_free[4] = ("sv", t_ap[1] + 1)
                            p_free[5] = ("sv", t_ap[1] + 1)

                        w1_free = [None, None]
                        w3_free = [None, None]
                        pt_free = [None, None]
                        for hc in range(NHC):
                            wp = hc % 2
                            t_w1 = kb.dma("gpsimd", f"w1{wp}", w1sl[wp][:, :], w1[hc], waits=[w1_free[wp]])
                            t_w3 = kb.dma("gpsimd", f"w3{wp}", w3sl[wp][:, :], w3[hc], waits=[w3_free[wp]])
                            for i in range(NTT):
                                gsel = (hc * NTT + i) % 2
                                Pg, Pl = (P[0], P[1]) if gsel == 0 else (P[2], P[3])
                                gi, li = (0, 1) if gsel == 0 else (2, 3)
                                kb.w("tensor", t_w1)
                                kb.w("tensor", t_w3)
                                kb.w("tensor", ln_ready2[i])
                                kb.w("tensor", p_free[gi])
                                kb.w("tensor", p_free[li])
                                for dc in range(NDC):
                                    kb.op("tensor", lambda e, o=Pg[:, 0:512],
                                          l=w1sl[wp][:, dc * 128:(dc + 1) * 128],
                                          r=ln_sb[:, dc * NT + i * TTS: dc * NT + (i + 1) * TTS],
                                          s=(dc == 0), t=(dc == NDC - 1):
                                          e.matmul(o, l, r, start=s, stop=t))
                                for dc in range(NDC):
                                    t_p = kb.op("tensor", lambda e, o=Pl[:, 0:512],
                                                l=w3sl[wp][:, dc * 128:(dc + 1) * 128],
                                                r=ln_sb[:, dc * NT + i * TTS: dc * NT + (i + 1) * TTS],
                                                s=(dc == 0), t=(dc == NDC - 1):
                                                e.matmul(o, l, r, start=s, stop=t),
                                                inc="sp" if dc == NDC - 1 else None)
                                if i == NTT - 1:
                                    w1_free[wp] = t_p
                                    w3_free[wp] = t_p
                                ep = i % 2
                                kb.w("scalar", t_p)
                                kb.w("scalar", ev_free[ep])
                                t_si = kb.op("scalar", lambda e, o=ev[ep][:, :], s=Pg[:, 0:512]:
                                             e.activation(o, s, AF.Silu), inc="ss")
                                p_free[gi] = ("ss", t_si[1] + 1)
                                kb.w("vector", t_si)
                                kb.w("vector", pt_free[ep])
                                t_h = kb.op("vector", lambda e, o=pt[ep][:, :], a=ev[ep][:, :],
                                            s=Pl[:, 0:512]: e.tensor_mul(o, a, s), inc="sv")
                                p_free[li] = ("sv", t_h[1] + 1)
                                ev_free[ep] = ("sv", t_h[1] + 1)
                                t_hd = kb.dma("sync", f"hd{ep}", hidd[hc * 128:(hc + 1) * 128, i * TTS:(i + 1) * TTS],
                                              pt[ep][:, :], waits=[t_h])
                                pt_free[ep] = t_hd

                        kb.barrier()

                    if stages >= 5:
                        with ExitStack() as se:
                            hblk = se.enter_context(nc.sbuf_tensor("hblk", [128, NHC * TTS], BF16))
                            w2sl = [se.enter_context(nc.sbuf_tensor(f"w2sl2{i}", [128, 64 * 128], BF16))
                                    for i in range(2)]
                            h3 = hidd.rearrange("(hc p) t -> p hc t", p=128)
                            hblk_free = None
                            w2_free = [None, None]
                            p_free = [None, None]
                            x1_free = [None, None]
                            yst_free = [None, None]
                            for i in range(NTT):
                                t_hb = kb.dma("sync", "hbl", hblk.rearrange("p (hc t) -> p hc t", t=TTS),
                                              h3[:, :, i * TTS:(i + 1) * TTS], waits=[hblk_free])
                                for oc in range(NOC):
                                    wp = oc % 2
                                    t_w = kb.dma("sync", f"wd{wp}", w2sl[wp][:, :], w2qd[oc], waits=[w2_free[wp]])
                                    xp = oc % 2
                                    t_x1 = kb.dma("sync", f"xl{xp}", x1c[xp][:, :],
                                                  x1d[oc * 128:(oc + 1) * 128, i * TTS:(i + 1) * TTS],
                                                  waits=[x1_free[xp]])
                                    pi = oc % 2
                                    kb.w("tensor", t_w)
                                    kb.w("tensor", t_hb)
                                    kb.w("tensor", p_free[pi])
                                    for a in range(NHC):
                                        t_p = kb.op("tensor", lambda e, o=P[pi][:, 0:512],
                                                    l=w2sl[wp][:, a * 128:(a + 1) * 128],
                                                    r=hblk[:, a * TTS:(a + 1) * TTS],
                                                    s=(a == 0), t=(a == NHC - 1):
                                                    e.matmul(o, l, r, start=s, stop=t),
                                                    inc="sp" if a == NHC - 1 else None)
                                    w2_free[wp] = t_p
                                    if oc == NOC - 1:
                                        hblk_free = t_p
                                    yp = oc % 2
                                    kb.w("vector", t_p)
                                    kb.w("vector", t_x1)
                                    kb.w("vector", yst_free[yp])
                                    t_y = kb.op("vector", lambda e, o=yst[yp][:, :], s=P[pi][:, 0:512],
                                                a=x1c[xp][:, :]: e.scalar_tensor_tensor(
                                                    o, s, W2SCALE, a, ALU.mult, ALU.add), inc="sv")
                                    p_free[pi] = ("sv", t_y[1] + 1)
                                    x1_free[xp] = ("sv", t_y[1] + 1)
                                    t_yo = kb.dma("sync", f"yw{yp}", y[oc * 128:(oc + 1) * 128, i * TTS:(i + 1) * TTS],
                                                  yst[yp][:, :], waits=[t_y])
                                    yst_free[yp] = t_yo

                            kb.w("sync", kb.tot("yw0"))
                            kb.w("sync", kb.tot("yw1"))

        with nc.Block() as blk:
            @blk.sync
            def _(e):
                for f in kb.ops["sync"]:
                    f(e)

            @blk.gpsimd
            def _(e):
                for f in kb.ops["gpsimd"]:
                    f(e)

            @blk.tensor
            def _(e):
                for f in kb.ops["tensor"]:
                    f(e)

            @blk.vector
            def _(e):
                for f in kb.ops["vector"]:
                    f(e)

            @blk.scalar
            def _(e):
                for f in kb.ops["scalar"]:
                    f(e)

    return nc


_NC_CACHE = None


def _get_nc():
    global _NC_CACHE
    if _NC_CACHE is None:
        _NC_CACHE = build()
    return _NC_CACHE


def prep_in_maps(L0_attn_x, L0_attn_Wq, L0_attn_Wk, L0_attn_Wv, L0_attn_Wo, L0_attn_bo,
                 L0_ffn_W1, L0_ffn_W2, L0_ffn_W3):
    xf = np.ascontiguousarray(np.asarray(L0_attn_x, np.float32))
    x2d = xf.reshape(B * L, D)

    def blockT(w, noc):
        wt = np.asarray(w, np.float32).T
        r = wt.reshape(NDC, 128, noc, 128).transpose(2, 1, 0, 3)
        return np.ascontiguousarray(r.reshape(noc, 128, NDC * 128))

    wqb = blockT(L0_attn_Wq, NOC)
    wkb = blockT(L0_attn_Wk, NOC)
    wvb = blockT(L0_attn_Wv, NOC)
    wob = blockT(L0_attn_Wo, NOC)
    w1b = blockT(L0_ffn_W1, NHC)
    w3b = blockT(L0_ffn_W3, NHC)
    w2b = np.ascontiguousarray(np.asarray(L0_ffn_W2, np.float32).T.reshape(64, 128, D))
    bob = np.ascontiguousarray(np.asarray(L0_attn_bo, np.float32).reshape(NOC, 128).T)

    in_maps = []
    for c in range(8):
        xc = np.ascontiguousarray(x2d[c * NT:(c + 1) * NT].T)
        in_maps.append({
            "x": xc, "wq": wqb, "wk": wkb, "wv": wvb, "wo": wob,
            "w1": w1b, "w3": w3b, "w2": w2b, "bo": bob,
        })
    return in_maps


def kernel(L0_attn_x, L0_attn_Wq, L0_attn_Wk, L0_attn_Wv, L0_attn_Wo, L0_attn_bo,
           L0_ffn_W1, L0_ffn_W2, L0_ffn_W3):
    in_maps = prep_in_maps(L0_attn_x, L0_attn_Wq, L0_attn_Wk, L0_attn_Wv, L0_attn_Wo,
                           L0_attn_bo, L0_ffn_W1, L0_ffn_W2, L0_ffn_W3)
    nc = _get_nc()
    res = run_bass_kernel_spmd(nc, in_maps, core_ids=list(range(8)))
    out = np.empty((B * L, D), np.float32)
    for c in range(8):
        out[c * NT:(c + 1) * NT] = np.asarray(res.results[c]["y"]).T
    return out.reshape(B, L, D)


# version 10
# speedup vs baseline: 1.2048x; 1.1099x over previous
import math
from contextlib import ExitStack

import numpy as np
import ml_dtypes

import concourse.bass as bass
from concourse.aot_env import allow_ici_all_to_all
import concourse.mybir as mybir
from concourse.bass_utils import run_bass_kernel_spmd

F32 = mybir.dt.float32
BF16 = mybir.dt.bfloat16
AF = mybir.ActivationFunctionType
ALU = mybir.AluOpType
bfdt = ml_dtypes.bfloat16

B, L, D, NH, HD, FF = 2, 8192, 2048, 16, 128, 8192
NT = 2048
TTS = 512
NTT = NT // TTS
NDC = D // 128
NOC = 16
NHC = FF // 128
EPS = 1e-5
LOG_RSHD = float(np.log(1.0 / np.sqrt(HD)))
W2SCALE = float(1.0 / np.sqrt(FF))
ENGS = ("sync", "gpsimd", "tensor", "vector", "scalar")


def _had(n):
    h = np.array([[1.0]], np.float32)
    while h.shape[0] < n:
        h = np.block([[h, h], [h, -h]])
    return h


class KB:

    def __init__(self, nc, st):
        self.nc = nc
        self.st = st
        self.ops = {e: [] for e in ENGS}
        self.sems = {}
        self.cnt = {}

    def mksem(self, *names):
        for n in names:
            if n not in self.sems:
                self.sems[n] = self.st.enter_context(self.nc.semaphore(n))
                self.cnt[n] = 0

    def w(self, eng, tok, future=0):
        if tok is None:
            return
        s, v = tok
        assert v <= self.cnt[s] + future, (eng, s, v, self.cnt[s])
        h = self.sems[s]
        self.ops[eng].append(lambda e, h=h, v=v: e.wait_ge(h, v))

    def op(self, eng, f, inc=None, amt=1):
        if inc is None:
            self.ops[eng].append(f)
            return None
        self.cnt[inc] += amt
        tok = (inc, self.cnt[inc])
        h = self.sems[inc]

        def g(e, f=f, h=h, a=amt):
            f(e).then_inc(h, a)
        self.ops[eng].append(g)
        return tok

    def dma(self, qeng, slot, dst, src, waits=(), **kw):
        for t in waits:
            self.w(qeng, t)
        sem = "d_" + slot
        self.mksem(sem)
        return self.op(qeng, lambda e, d=dst, s=src, kw=kw: e.dma_start(out=d, in_=s, **kw),
                       inc=sem, amt=16)

    def tot(self, slot):
        return ("d_" + slot, self.cnt["d_" + slot])

    def echo(self):
        return self.op("vector", lambda e: e.tensor_copy(
            self.escr[0:1, 0:1], self.escr[0:1, 1:2]), inc="sv")

    def barrier(self):
        base = self.cnt["sb"]
        for e in ENGS:
            self.op(e, lambda en: en.nop(), inc="sb")
        toks = [("sb", base + len(ENGS))]
        toks += [(s, self.cnt[s]) for s in self.sems if s.startswith("d_")]
        for e in ENGS:
            for t in toks:
                self.w(e, t)


def build(dbg=False, stages=5, p1a=True, p1m=3, p1post=3):
    nc = bass.Bass("TRN2", target_bir_lowering=False, debug=False)
    DBG = "ExternalOutput" if dbg else "Internal"

    x = nc.dram_tensor("x", [D, NT], F32, kind="ExternalInput").ap()
    y = nc.dram_tensor("y", [D, NT], F32, kind="ExternalOutput").ap()
    wq = nc.dram_tensor("wq", [NOC, 128, D], F32, kind="ExternalInput").ap()
    wk = nc.dram_tensor("wk", [NOC, 128, D], F32, kind="ExternalInput").ap()
    wv = nc.dram_tensor("wv", [NOC, 128, D], F32, kind="ExternalInput").ap()
    wo = nc.dram_tensor("wo", [NOC, 128, D], F32, kind="ExternalInput").ap()
    w1 = nc.dram_tensor("w1", [NHC, 128, D], F32, kind="ExternalInput").ap()
    w3 = nc.dram_tensor("w3", [NHC, 128, D], F32, kind="ExternalInput").ap()
    w2 = nc.dram_tensor("w2", [64, 128, D], F32, kind="ExternalInput").ap()
    bo = nc.dram_tensor("bo", [128, NOC], F32, kind="ExternalInput").ap()

    cc_qk_in = nc.dram_tensor("cc_qk_in", [2 * D, NT], BF16, kind="Internal").ap()
    cc_qk_out = nc.dram_tensor("cc_qk_out", [2 * D, NT], BF16, kind="Internal").ap()
    cc_v_in = nc.dram_tensor("cc_v_in", [D, 16 * 129], BF16, kind="Internal").ap()
    cc_vout = nc.dram_tensor("cc_vout", [D, 16 * 129], BF16, kind="Internal").ap()
    cc_o_in = nc.dram_tensor("cc_o_in", [D, NT], BF16, kind="Internal").ap()
    cc_o_out = nc.dram_tensor("cc_o_out", [D, NT], BF16, kind="Internal").ap()
    w2m = nc.dram_tensor("w2m", [64, 128 * D], BF16, kind=DBG).ap()
    w2qd = nc.dram_tensor("w2qd", [NOC, 128, 64 * 128], BF16, kind=DBG).ap()
    x1d = nc.dram_tensor("x1d", [D, NT], F32, kind=DBG).ap()
    hidd = nc.dram_tensor("hidd", [FF, NT], BF16, kind=DBG).ap()

    c_ident = nc.inline_tensor(np.eye(128, dtype=bfdt), name="c_ident").ap()
    c_h64 = nc.inline_tensor(_had(64).astype(bfdt), name="c_h64").ap()
    c_h128 = nc.inline_tensor(_had(128).astype(bfdt), name="c_h128").ap()
    tri = (np.arange(128)[:, None] <= np.arange(128)[None, :]).astype(bfdt)
    c_tri = nc.inline_tensor(tri, name="c_tri").ap()
    c_onesc = nc.inline_tensor(np.ones((128, 1), np.float32), name="c_onesc").ap()
    c_onesr = nc.inline_tensor(np.ones((1, 128), np.float32), name="c_onesr").ap()

    with ExitStack() as st:
        E = st.enter_context
        kb = KB(nc, st)
        kb.mksem("scc", "sp", "sv", "ss", "sb")

        ident = E(nc.sbuf_tensor("ident", [128, 128], BF16))
        h64s = E(nc.sbuf_tensor("h64s", [64, 64], BF16))
        h128s = E(nc.sbuf_tensor("h128s", [128, 128], BF16))
        tris = E(nc.sbuf_tensor("tris", [128, 128], BF16))
        onesc = E(nc.sbuf_tensor("onesc", [128, 1], F32))
        onesr = E(nc.sbuf_tensor("onesr", [1, 128], F32))
        bos = E(nc.sbuf_tensor("bos", [128, NOC], F32))
        r_mu = E(nc.sbuf_tensor("r_mu", [1, 512], F32))
        r_mu2 = E(nc.sbuf_tensor("r_mu2", [1, 512], F32))
        r_var = E(nc.sbuf_tensor("r_var", [1, 512], F32))
        r_lnv = E(nc.sbuf_tensor("r_lnv", [1, 512], F32))
        r_ab = E(nc.sbuf_tensor("r_ab", [1, 1024], F32))
        ev = [E(nc.sbuf_tensor(f"ev{i}", [128, 512], F32)) for i in range(2)]
        sq = [E(nc.sbuf_tensor(f"sq{i}", [128, 512], F32)) for i in range(2)]
        tmp = [E(nc.sbuf_tensor(f"tmp{i}", [128, 512], F32)) for i in range(2)]
        qkst = [E(nc.sbuf_tensor(f"qkst{i}", [128, NT], BF16)) for i in range(2)]
        vst = [E(nc.sbuf_tensor(f"vst{i}", [128, 16 * 129], BF16)) for i in range(2)]
        pt = [E(nc.sbuf_tensor(f"pt{i}", [128, 512], BF16)) for i in range(2)]
        otm = [E(nc.sbuf_tensor(f"otm{i}", [128, 128], BF16)) for i in range(2)]
        rcs = E(nc.sbuf_tensor("rcs", [128, 4], F32))
        kb_escr = E(nc.sbuf_tensor("kb_escr", [1, 2], F32))
        x1c = [E(nc.sbuf_tensor(f"x1c{i}", [128, 512], F32)) for i in range(2)]
        yst = [E(nc.sbuf_tensor(f"yst{i}", [128, 512], F32)) for i in range(2)]

        P = [E(nc.psum_tensor(f"P{i}", [128, 512], F32)) for i in range(6)]
        TPBS = [E(nc.psum_tensor(f"TPB{i}", [128, 128], BF16)) for i in range(2)]

        kb.escr = kb_escr
        for dst, src in ((ident, c_ident), (h64s, c_h64), (h128s, c_h128), (tris, c_tri),
                         (onesc, c_onesc), (onesr, c_onesr), (bos, bo)):
            kb.dma("sync", "cst", dst[:, :], src)
        t_init = kb.tot("cst")
        kb.op("vector", lambda e: e.memset(kb_escr[0:1, :], 0.0))
        for i in range(2):
            tok = kb.op("vector", lambda e, b=vst[i]: e.memset(b[:, :], 1.0), inc="sv")
        t_vst_ones = tok
        for e in ENGS:
            kb.w(e, t_init)
        kb.w("scalar", t_vst_ones)
        kb.w("sync", t_vst_ones)

        def rows_ln(stat_a, stat_b, nfeat, exp_bias, want_b):
            inv = 1.0 / nfeat
            kb.op("vector", lambda e: e.tensor_scalar_mul(r_mu[0:1, :], stat_a[0:1, 0:512], inv))
            kb.op("vector", lambda e: e.tensor_mul(r_mu2[0:1, :], r_mu[0:1, :], r_mu[0:1, :]))
            kb.op("vector", lambda e: e.scalar_tensor_tensor(
                r_var[0:1, :], stat_b[0:1, 0:512], inv, r_mu2[0:1, :], ALU.mult, ALU.subtract))
            kb.op("vector", lambda e: e.tensor_scalar_add(
                r_var[0:1, :], r_var[0:1, :], EPS))
            tv = kb.echo()
            kb.w("scalar", tv)
            tl = kb.op("scalar", lambda e: e.activation(r_lnv[0:1, :], r_var[0:1, :], AF.Ln),
                       inc="ss")
            kb.w("vector", tl)
            if exp_bias != 0.0:
                kb.op("vector", lambda e, bb=exp_bias: e.tensor_scalar(
                    r_lnv[0:1, :], r_lnv[0:1, :], -0.5, bb, ALU.mult, ALU.add))
            else:
                kb.op("vector", lambda e: e.tensor_scalar_mul(
                    r_lnv[0:1, :], r_lnv[0:1, :], -0.5))
            tm = kb.echo()
            kb.w("scalar", tm)
            ta = kb.op("scalar", lambda e: e.activation(
                r_ab[0:1, 0:512], r_lnv[0:1, :], AF.Exp), inc="ss")
            if want_b:
                kb.w("vector", ta)
                kb.op("vector", lambda e: e.tensor_mul(r_ab[0:1, 512:1024], r_mu[0:1, :],
                                                       r_ab[0:1, 0:512]))
                kb.op("vector", lambda e: e.tensor_scalar_mul(
                    r_ab[0:1, 512:1024], r_ab[0:1, 512:1024], -1.0))
                tb = kb.echo()
                return ta, tb
            return ta, None

        with ExitStack() as sa:
            ln_sb = sa.enter_context(nc.sbuf_tensor("ln_sb", [128, NDC * NT], BF16))
            xt = [sa.enter_context(nc.sbuf_tensor(f"xt{i}", [128, NDC * TTS], F32))
                  for i in range(2)]
            wsl = [sa.enter_context(nc.sbuf_tensor(f"wsl{i}", [128, D], BF16))
                   for i in range(2)]

            x3 = x.rearrange("(dc p) t -> p dc t", p=128)
            ln_ready = [None] * NTT
            xt_free = [None, None]
            rows_fr = [None]
            p_free = [None] * 6
            tpb_free = None
            ev_free = [None, None]
            sq_free = [None, None]
            pt_free = [None, None]
            tmp_free = [None, None]

            for i in range(NTT if p1a else 0):
                ip = i % 2
                t_x = kb.dma("sync", f"xt{ip}", xt[ip].rearrange("p (dc t) -> p dc t", t=TTS),
                             x3[:, :, i * TTS:(i + 1) * TTS], waits=[xt_free[ip]])
                kb.w("tensor", t_x)
                kb.w("tensor", p_free[0])
                kb.w("tensor", p_free[1])
                kb.w("vector", t_x)
                sq_toks = []
                for dc in range(NDC):
                    xs = xt[ip][:, dc * TTS:(dc + 1) * TTS]
                    kb.op("tensor", lambda e, o=P[0][0:1, 0:512], l=onesc[:, 0:1], r=xs,
                          s=(dc == 0), t=(dc == NDC - 1): e.matmul(o, l, r, start=s, stop=t))
                    dp = dc % 2
                    if dc >= 2:
                        kb.w("vector", sq_toks[dc - 2][1])
                    kb.op("vector", lambda e, o=sq[dp][:, :], a=xs: e.tensor_mul(o, a, a))
                    tsq = kb.echo()
                    kb.w("tensor", tsq)
                    tmm = kb.op("tensor", lambda e, o=P[1][0:1, 0:512], l=onesc[:, 0:1],
                                r=sq[dp][:, :], s=(dc == 0), t=(dc == NDC - 1):
                                e.matmul(o, l, r, start=s, stop=t), inc="sp")
                    sq_toks.append((tsq, tmm))
                t_stats = sq_toks[-1][1]
                kb.w("vector", t_stats)
                kb.w("vector", rows_fr[0])
                ta, tb = rows_ln(P[0], P[1], D, 0.0, True)
                kb.w("tensor", tb)
                kb.w("tensor", p_free[2])
                kb.w("tensor", p_free[3])
                kb.op("tensor", lambda e: e.matmul(P[2][:, 0:512], onesr[0:1, :],
                                                   r_ab[0:1, 0:512], start=True, stop=True))
                t_bc = kb.op("tensor", lambda e: e.matmul(P[3][:, 0:512], onesr[0:1, :],
                                                          r_ab[0:1, 512:1024], start=True,
                                                          stop=True), inc="sp")
                rows_fr[0] = t_bc
                p_free[0] = t_bc
                p_free[1] = t_bc
                kb.w("vector", t_bc)
                t_ap = None
                for dc in range(NDC):
                    xs = xt[ip][:, dc * TTS:(dc + 1) * TTS]
                    dp = dc % 2
                    if dc >= 2:
                        kb.w("vector", tmp_free[dp])
                    kb.op("vector", lambda e, o=tmp[dp][:, :], a=xs: e.tensor_mul(o, a, P[2][:, 0:512]))
                    dst = ln_sb[:, dc * NT + i * TTS: dc * NT + (i + 1) * TTS]
                    t_ap = kb.op("vector", lambda e, o=dst, a=tmp[dp][:, :]:
                                 e.tensor_add(o, a, P[3][:, 0:512]), inc="sv")
                    tmp_free[dp] = t_ap
                ln_ready[i] = kb.echo()
                xt_free[ip] = t_ap
                p_free[2] = ("sv", t_ap[1] + 1)
                p_free[3] = ("sv", t_ap[1] + 1)

            wt_free = [None, None]
            qkst_free = [None, None]
            vst_free = [None, None]
            tpb_free = [None, None]
            gtile = 0

            def post_1b(ctx):
                m, oc, i, ip, pi, t_p, op = ctx
                PP = P[pi]
                if m < 2:
                    kb.w("scalar", t_p)
                    kb.w("scalar", ev_free[ip])
                    t_ev = kb.op("scalar", lambda e, o=ev[ip][:, :], s=PP[:, 0:512]:
                                 e.copy(o, s), inc="ss")
                    p_free[pi] = ("ss", t_ev[1] + 1)
                    kb.w("vector", t_ev)
                    kb.w("vector", sq_free[ip])
                    kb.op("vector", lambda e, o=sq[ip][:, :], a=ev[ip][:, :]:
                          e.tensor_mul(o, a, a))
                    t_sq = kb.echo()
                    kb.w("tensor", t_ev)
                    kb.w("tensor", p_free[2])
                    kb.w("tensor", p_free[3])
                    kb.op("tensor", lambda e, r=ev[ip][:, :]: e.matmul(
                        P[2][0:1, 0:512], onesc[:, 0:1], r, start=True, stop=True))
                    kb.w("tensor", t_sq)
                    t_st = kb.op("tensor", lambda e, r=sq[ip][:, :]: e.matmul(
                        P[3][0:1, 0:512], onesc[:, 0:1], r, start=True, stop=True),
                        inc="sp")
                    sq_free[ip] = t_st
                    kb.w("vector", t_st)
                    kb.w("vector", rows_fr[0])
                    ta, tb = rows_ln(P[2], P[3], HD, LOG_RSHD if m == 0 else 0.0,
                                     m == 1)
                    kb.w("tensor", tb if m == 1 else ta)
                    kb.w("tensor", p_free[4])
                    kb.w("tensor", p_free[5] if m == 1 else None)
                    if m == 1:
                        kb.op("tensor", lambda e: e.matmul(
                            P[4][:, 0:512], onesr[0:1, :], r_ab[0:1, 0:512],
                            start=True, stop=True))
                        t_bc = kb.op("tensor", lambda e: e.matmul(
                            P[5][:, 0:512], onesr[0:1, :], r_ab[0:1, 512:1024],
                            start=True, stop=True), inc="sp")
                    else:
                        t_bc = kb.op("tensor", lambda e: e.matmul(
                            P[4][:, 0:512], onesr[0:1, :], r_ab[0:1, 0:512],
                            start=True, stop=True), inc="sp")
                    rows_fr[0] = t_bc
                    p_free[2] = t_bc
                    p_free[3] = t_bc
                    kb.w("vector", t_bc)
                    kb.w("vector", qkst_free[op])
                    dst = qkst[op][:, i * TTS:(i + 1) * TTS]
                    if m == 0:
                        t_sg = kb.op("vector", lambda e, o=dst, a=ev[ip][:, :]:
                                     e.tensor_mul(o, a, P[4][:, 0:512]), inc="sv")
                    else:
                        kb.w("vector", tmp_free[0])
                        kb.op("vector", lambda e, a=ev[ip][:, :]:
                              e.tensor_mul(tmp[0][:, :], a, P[4][:, 0:512]))
                        t_sg = kb.op("vector", lambda e, o=dst:
                                     e.tensor_add(o, tmp[0][:, :], P[5][:, 0:512]),
                                     inc="sv")
                        tmp_free[0] = t_sg
                    ev_free[ip] = ("sv", t_sg[1] + 1)
                    p_free[4] = ("sv", t_sg[1] + 1)
                    p_free[5] = ("sv", t_sg[1] + 1)
                    t_last = t_sg
                else:
                    kb.w("scalar", t_p)
                    kb.w("scalar", pt_free[ip])
                    t_vb = kb.op("scalar", lambda e, o=pt[ip][:, :], s=PP[:, 0:512]:
                                 e.copy(o, s), inc="ss")
                    p_free[pi] = ("ss", t_vb[1] + 1)
                    for qb in range(4):
                        tb_ = qb % 2
                        kb.w("tensor", t_vb)
                        kb.w("tensor", tpb_free[tb_], future=1)
                        t_tp = kb.op("tensor", lambda e, s=pt[ip][:, qb * 128:(qb + 1) * 128],
                                     o=TPBS[tb_][:, :]:
                                     e.transpose(o, s, ident[:, :]), inc="sp")
                        if qb == 3:
                            pt_free[ip] = t_tp
                        kb.w("scalar", t_tp)
                        kb.w("scalar", vst_free[op])
                        col = (i * 4 + qb) * 129
                        t_te = kb.op("scalar", lambda e, o=vst[op][:, col:col + 128],
                                     s=TPBS[tb_][:, :]: e.copy(o, s), inc="ss")
                        tpb_free[tb_] = ("ss", t_te[1] + 1)
                        t_last = t_te
                if i == NTT - 1:
                    if m == 0:
                        ro = (oc // 2) * 512 + (oc % 2) * 128
                        dst = cc_qk_in[ro:ro + 128, :]
                        t_out = kb.dma("sync", f"qs{op}", dst, qkst[op][:, :], waits=[t_last])
                        qkst_free[op] = t_out
                    elif m == 1:
                        ro = (oc // 2) * 512 + 256 + (oc % 2) * 128
                        dst = cc_qk_in[ro:ro + 128, :]
                        t_out = kb.dma("sync", f"qs{op}", dst, qkst[op][:, :], waits=[t_last])
                        qkst_free[op] = t_out
                    else:
                        ro = (oc // 2) * 256 + (oc % 2) * 128
                        dst = cc_v_in[ro:ro + 128, :]
                        t_out = kb.dma("sync", f"vs{op}", dst, vst[op][:, :], waits=[t_last])
                        vst_free[op] = t_out

            pending = None
            for m, wten in enumerate((wq, wk, wv)[:p1m]):
                for oc in range(NOC):
                    wp = oc % 2
                    t_w = kb.dma("gpsimd", f"ws{wp}", wsl[wp][:, :], wten[oc], waits=[wt_free[wp]])
                    op = oc % 2
                    for i in range(NTT):
                        ip = i % 2
                        pi = gtile % 2
                        gtile += 1
                        PP = P[pi]
                        kb.w("tensor", t_w)
                        kb.w("tensor", ln_ready[i])
                        kb.w("tensor", p_free[pi], future=2)
                        for dc in range(NDC):
                            t_p = kb.op("tensor", lambda e, o=PP[:, 0:512],
                                        l=wsl[wp][:, dc * 128:(dc + 1) * 128],
                                        r=ln_sb[:, dc * NT + i * TTS: dc * NT + (i + 1) * TTS],
                                        s=(dc == 0), t=(dc == NDC - 1):
                                        e.matmul(o, l, r, start=s, stop=t),
                                        inc="sp" if dc == NDC - 1 else None)
                        wt_free[wp] = t_p
                        cur = (m, oc, i, ip, pi, t_p, op)
                        if pending is not None:
                            post_1b(pending)
                        pending = cur
            post_1b(pending)

            kb.mksem("d_qs0", "d_qs1", "d_vs0", "d_vs1")
            t_p1_dma = [kb.tot("qs0"), kb.tot("qs1"), kb.tot("vs0"), kb.tot("vs1")]

            kb.barrier()

        if stages >= 2:
            for t in t_p1_dma:
                kb.w("gpsimd", t)
            def a2a(e, i_, o_):
                with allow_ici_all_to_all():
                    return e.collective_compute(
                        "AllToAll", ALU.bypass,
                        replica_groups=[[0, 1, 2, 3, 4, 5, 6, 7]],
                        ins=[i_.opt()], outs=[o_.opt()])
            t_cc_qk = kb.op("gpsimd", lambda e: a2a(e, cc_qk_in, cc_qk_out), inc="scc")
            t_cc_v = kb.op("gpsimd", lambda e: a2a(e, cc_v_in, cc_vout), inc="scc")

            with ExitStack() as sb:
                w2sl = [sb.enter_context(nc.sbuf_tensor(f"w2sl{i}", [64, 8192], BF16))
                        for i in range(2)]
                f1st = sb.enter_context(nc.sbuf_tensor("f1st", [64, 8192], BF16))
                w2f = [sb.enter_context(nc.sbuf_tensor(f"w2f{i}", [128, D], BF16))
                       for i in range(2)]
                f2st = sb.enter_context(nc.sbuf_tensor("f2st", [128, D], BF16))
                w2flat = w2.rearrange("a b d -> a (b d)")

                sl_free = [None, None]
                p_free = [None, None]
                f1_free = None
                for s in range(32):
                    sp_ = s % 2
                    t_w = kb.dma("gpsimd", f"fw{sp_}", w2sl[sp_][:, :],
                                 w2flat[:, s * 8192:(s + 1) * 8192], waits=[sl_free[sp_]])
                    seg_toks = []
                    for n in range(16):
                        pi = n % 2
                        kb.w("tensor", t_w)
                        kb.w("tensor", p_free[pi])
                        t_mm = kb.op("tensor", lambda e, o=P[pi][0:64, 0:512],
                                     r=w2sl[sp_][:, n * 512:(n + 1) * 512]:
                                     e.matmul(o, h64s[0:64, 0:64], r, start=True, stop=True),
                                     inc="sp")
                        kb.w("scalar", t_mm)
                        kb.w("scalar", f1_free if n == 0 else None)
                        t_ev = kb.op("scalar", lambda e, o=f1st[:, n * 512:(n + 1) * 512],
                                     s2=P[pi][0:64, 0:512]:
                                     e.copy(o, s2), inc="ss")
                        p_free[pi] = t_ev
                        seg_toks.append(t_ev)
                    sl_free[sp_] = seg_toks[-1]
                    f1_free = kb.dma("sync", "f1w", w2m[:, s * 8192:(s + 1) * 8192], f1st[:, :],
                                     waits=[seg_toks[-1]])
                t_w2m = kb.tot("f1w")
                f2_free = None
                wf_free = [None, None]
                for a in range(64):
                    ap_ = a % 2
                    t_w = kb.dma("sync", f"fl{ap_}", w2f[ap_][:, :],
                                 w2m[a:a + 1, :].rearrange("o (b d) -> (o b) d", b=128),
                                 waits=[wf_free[ap_], t_w2m if a == 0 else None])
                    seg_toks = []
                    for n in range(4):
                        pi = n % 2
                        kb.w("tensor", t_w)
                        kb.w("tensor", p_free[pi])
                        t_mm = kb.op("tensor", lambda e, o=P[pi][:, 0:512],
                                     r=w2f[ap_][:, n * 512:(n + 1) * 512]:
                                     e.matmul(o, h128s[:, :], r, start=True, stop=True), inc="sp")
                        kb.w("scalar", t_mm)
                        kb.w("scalar", f2_free if n == 0 else None)
                        t_ev = kb.op("scalar", lambda e, o=f2st[:, n * 512:(n + 1) * 512],
                                     s2=P[pi][:, 0:512]: e.copy(o, s2), inc="ss")
                        p_free[pi] = ("ss", t_ev[1] + 1)
                        seg_toks.append(t_ev)
                    wf_free[ap_] = seg_toks[-1]
                    f2_free = kb.dma(
                        "sync", "f2w",
                        w2qd.rearrange("oc b f -> b oc f")[:, :, a * 128:(a + 1) * 128],
                        f2st.rearrange("b (oc o) -> b oc o", oc=16),
                        waits=[seg_toks[-1]])
                t_w2q = kb.tot("f2w")

                kb.barrier()

            if stages >= 3:
                with ExitStack() as sc:
                    kbuf = [sc.enter_context(nc.sbuf_tensor(f"kbuf{i}", [128, L], BF16))
                            for i in range(2)]
                    vbuf = [sc.enter_context(nc.sbuf_tensor(f"vbuf{i}", [128, 64 * 129], BF16))
                            for i in range(2)]
                    qbuf = [sc.enter_context(nc.sbuf_tensor(f"qbuf{i}", [128, L], BF16))
                            for i in range(2)]
                    osta = [sc.enter_context(nc.sbuf_tensor(f"osta{i}", [128, L], BF16))
                            for i in range(2)]

                    pair_cons = [None] * 4
                    osta_free = [None, None]
                    ld_toks = [None] * 4

                    qko3 = cc_qk_out.rearrange("(b j four p) t -> four b p j t",
                                               b=2, j=4, four=4)
                    vo3 = cc_vout.rearrange("(b j hl p) f -> hl b p j f", b=2, j=4, hl=2)

                    def emit_pair_load(p):
                        hl, b = p // 2, p % 2
                        bp = p % 2
                        w_ = [pair_cons[p - 2] if p >= 2 else None]
                        t1 = kb.dma("sync", f"kb{bp}",
                                    kbuf[bp].rearrange("p (j t) -> p j t", j=4),
                                    qko3[2 + hl, b], waits=w_)
                        t2 = kb.dma("sync", f"vb{bp}",
                                    vbuf[bp].rearrange("p (j f) -> p j f", j=4),
                                    vo3[hl, b])
                        t3 = kb.dma("sync", f"qb{bp}",
                                    qbuf[bp].rearrange("p (j t) -> p j t", j=4),
                                    qko3[hl, b])
                        ld_toks[p] = (t1, t2, t3)

                    kb.w("sync", t_cc_qk)
                    kb.w("sync", t_cc_v)
                    emit_pair_load(0)
                    emit_pair_load(1)
                    oin = cc_o_in.rearrange("(b j hl p) t -> b hl p j t", b=2, j=4, hl=2)

                    pst_free = [None, None]
                    pt_free = [None, None]
                    acc_free = [None] * 4
                    tpb_free = [None, None]
                    otm_free = [None, None]

                    for p in range(4):
                        hl, b = p // 2, p % 2
                        bp = p % 2
                        for t in ld_toks[p]:
                            kb.w("tensor", t)
                        t_pe_last = None
                        for qg in range(16):
                            qbase = qg * 512
                            nkc = 4 * qg + 4
                            stop_tok = [None] * 4
                            st_toks = [None] * nkc

                            def emit_st(kc, qg=qg, qbase=qbase, nkc=nkc):
                                par = kc % 2
                                diag = kc >= 4 * qg
                                qoff = (kc - 4 * qg) * 128 if diag else 0
                                mov = 512 - qoff
                                kb.w("tensor", pst_free[par], future=1)
                                st_toks[kc] = kb.op("tensor", lambda e, o=P[par][:, 0:mov],
                                                    l=kbuf[bp][:, kc * 128:(kc + 1) * 128],
                                                    r=qbuf[bp][:, qbase + qoff: qbase + 512]:
                                                    e.matmul(o, l, r, start=True, stop=True),
                                                    inc="sp")

                            emit_st(0)
                            for kc in range(nkc):
                                par = kc % 2
                                diag = kc >= 4 * qg
                                qoff = (kc - 4 * qg) * 128 if diag else 0
                                mov = 512 - qoff
                                if kc + 1 < nkc:
                                    emit_st(kc + 1)
                                t_st = st_toks[kc]
                                kb.w("scalar", t_st)
                                kb.w("scalar", pt_free[par])
                                t_exp = kb.op("scalar", lambda e, o=pt[par][:, 0:mov],
                                              s=P[par][:, 0:mov]:
                                              e.activation(o, s, AF.Exp), inc="ss")
                                pst_free[par] = ("ss", t_exp[1] + 1)
                                if diag:
                                    kb.w("vector", t_exp)
                                    kb.op("vector", lambda e, o=pt[par][:, 0:128]:
                                          e.tensor_mul(o, o, tris[:, :]))
                                    t_msk = kb.echo()
                                    kb.w("tensor", t_msk)
                                else:
                                    kb.w("tensor", t_exp)
                                qb0 = (kc - 4 * qg) if diag else 0
                                for qb in range(qb0, 4):
                                    if kc == 0:
                                        kb.w("tensor", acc_free[qb])
                                    last = qb == 3
                                    t_pv = kb.op("tensor", lambda e, o=P[2 + qb][:, 0:129],
                                                 l=pt[par][:, qb * 128 - qoff: qb * 128 - qoff + 128],
                                                 r=vbuf[bp][:, kc * 129:(kc + 1) * 129],
                                                 s=(kc == 0), t=(kc == 4 * qg + qb):
                                                 e.matmul(o, l, r, start=s, stop=t),
                                                 inc="sp" if last else None)
                                for qb in range(qb0, 4):
                                    if kc == 4 * qg + qb:
                                        stop_tok[qb] = t_pv
                                pt_free[par] = t_pv
                                t_pe_last = t_pv
                            for qb in range(4):
                                po = qb % 2
                                tb_ = qb % 2
                                kb.w("vector", stop_tok[qb])
                                kb.op("vector", lambda e, s=P[2 + qb][:, 128:129], o=rcs[:, qb:qb + 1]:
                                      e.reciprocal(o, s))
                                t_rc = kb.echo()
                                kb.w("scalar", t_rc)
                                kb.w("scalar", otm_free[po])
                                t_nm = kb.op("scalar", lambda e, o=otm[po][:, :],
                                             s=P[2 + qb][:, 0:128], r=rcs[:, qb:qb + 1]:
                                             e.activation(o, s, AF.Copy, scale=r), inc="ss")
                                kb.w("tensor", t_nm)
                                kb.w("tensor", tpb_free[tb_])
                                t_tp = kb.op("tensor", lambda e, s=otm[po][:, :],
                                             o=TPBS[tb_][:, :]:
                                             e.transpose(o, s, ident[:, :]), inc="sp")
                                kb.w("scalar", t_tp)
                                kb.w("scalar", osta_free[bp] if (qg == 0 and qb == 0) else None)
                                col = qbase + qb * 128
                                t_te = kb.op("scalar", lambda e, o=osta[bp][:, col:col + 128],
                                             s=TPBS[tb_][:, :]: e.copy(o, s), inc="ss")
                                acc_free[qb] = t_te
                                tpb_free[tb_] = ("ss", t_te[1] + 1)
                                otm_free[po] = t_tp
                        pair_cons[p] = t_pe_last
                        t_o = kb.dma("sync", f"ost{bp}", oin[b, hl],
                                     osta[bp].rearrange("p (j t) -> p j t", j=4),
                                     waits=[t_te])
                        osta_free[bp] = t_o
                        if p + 2 < 4:
                            emit_pair_load(p + 2)

                    kb.w("gpsimd", kb.tot("ost0"))
                    kb.w("gpsimd", kb.tot("ost1"))
                    t_cc_o = kb.op("gpsimd", lambda e: a2a(e, cc_o_in, cc_o_out), inc="scc")
                    kb.barrier()

                if stages >= 4:
                    with ExitStack() as sd_:
                        ln_sb = sd_.enter_context(nc.sbuf_tensor("ln2_sb", [128, NDC * NT], BF16))
                        xt = [sd_.enter_context(nc.sbuf_tensor(f"xt4{i}", [128, NDC * TTS], F32))
                              for i in range(1)]
                        ofm = [sd_.enter_context(nc.sbuf_tensor(f"ofm{i}", [128, NOC * TTS], BF16))
                               for i in range(2)]
                        wsl = [sd_.enter_context(nc.sbuf_tensor(f"wsl4{i}", [128, D], BF16))
                               for i in range(2)]
                        w1sl = [sd_.enter_context(nc.sbuf_tensor(f"w1sl{i}", [128, D], BF16))
                                for i in range(2)]
                        w3sl = [sd_.enter_context(nc.sbuf_tensor(f"w3sl{i}", [128, D], BF16))
                                for i in range(2)]

                        x3 = x.rearrange("(dc p) t -> p dc t", p=128)
                        mo3 = cc_o_out.rearrange("(h p) t -> p h t", p=128)
                        ln_ready2 = [None] * NTT
                        p_free = [None] * 6
                        xt_free = [None]
                        ofm_free = [None, None]
                        wt_free = [None, None]
                        ev_free = [None, None]
                        sq_free = [None, None]
                        tmp_free = [None, None]
                        rows_free = None
                        t_x1st = None
                        for i in range(NTT):
                            ip = 0
                            op_ = i % 2
                            t_x = kb.dma("sync", "xt0", xt[ip].rearrange("p (dc t) -> p dc t", t=TTS),
                                         x3[:, :, i * TTS:(i + 1) * TTS], waits=[xt_free[ip], t_x1st])
                            t_of = kb.dma("sync", f"of{op_}", ofm[op_].rearrange("p (h t) -> p h t", t=TTS),
                                          mo3[:, :, i * TTS:(i + 1) * TTS],
                                          waits=[ofm_free[op_], t_cc_o])
                            x1_toks = []
                            for oc in range(NOC):
                                wp = oc % 2
                                t_w = kb.dma("gpsimd", f"ws{wp}", wsl[wp][:, :], wo[oc], waits=[wt_free[wp]])
                                pi = oc % 2
                                kb.w("tensor", t_w)
                                kb.w("tensor", t_of)
                                kb.w("tensor", p_free[pi])
                                for dc in range(NDC):
                                    t_p = kb.op("tensor", lambda e, o=P[pi][:, 0:512],
                                                l=wsl[wp][:, dc * 128:(dc + 1) * 128],
                                                r=ofm[op_][:, dc * TTS:(dc + 1) * TTS],
                                                s=(dc == 0), t=(dc == NDC - 1):
                                                e.matmul(o, l, r, start=s, stop=t),
                                                inc="sp" if dc == NDC - 1 else None)
                                wt_free[wp] = t_p
                                kb.w("vector", t_p)
                                kb.w("vector", t_x)
                                xs = xt[ip][:, oc * TTS:(oc + 1) * TTS]
                                kb.op("vector", lambda e, o=xs, s=P[pi][:, 0:512],
                                      bb=bos[:, oc:oc + 1]:
                                      e.scalar_tensor_tensor(o, s, bb, o, ALU.add, ALU.add))
                                sp2 = oc % 2
                                kb.w("vector", sq_free[sp2])
                                kb.op("vector", lambda e, o=sq[sp2][:, :], a=xs:
                                      e.tensor_mul(o, a, a))
                                t_x1 = kb.echo()
                                p_free[pi] = t_x1
                                x1_toks.append(t_x1)
                                t_x1st = kb.dma("sync", "x1s", x1d[oc * 128:(oc + 1) * 128, i * TTS:(i + 1) * TTS],
                                                xs, waits=[t_x1])
                                kb.w("tensor", t_x1)
                                if oc == 0:
                                    kb.w("tensor", p_free[2])
                                    kb.w("tensor", p_free[3])
                                kb.op("tensor", lambda e, r=xs, s=(oc == 0), t=(oc == NOC - 1):
                                      e.matmul(P[2][0:1, 0:512], onesc[:, 0:1], r, start=s, stop=t))
                                t_st = kb.op("tensor", lambda e, r=sq[sp2][:, :], s=(oc == 0),
                                             t=(oc == NOC - 1):
                                             e.matmul(P[3][0:1, 0:512], onesc[:, 0:1], r, start=s, stop=t),
                                             inc="sp")
                                sq_free[sp2] = t_st
                            t_stats = t_st
                            kb.w("vector", t_stats)
                            kb.w("vector", rows_free)
                            ta, tb = rows_ln(P[2], P[3], D, 0.0, True)
                            kb.w("tensor", tb)
                            kb.w("tensor", p_free[4])
                            kb.w("tensor", p_free[5])
                            kb.op("tensor", lambda e: e.matmul(P[4][:, 0:512], onesr[0:1, :],
                                                               r_ab[0:1, 0:512], start=True, stop=True))
                            t_bc = kb.op("tensor", lambda e: e.matmul(P[5][:, 0:512], onesr[0:1, :],
                                                                      r_ab[0:1, 512:1024], start=True,
                                                                      stop=True), inc="sp")
                            rows_free = t_bc
                            p_free[2] = t_bc
                            p_free[3] = t_bc
                            kb.w("vector", t_bc)
                            t_ap = None
                            for dc in range(NDC):
                                xs = xt[ip][:, dc * TTS:(dc + 1) * TTS]
                                dp = dc % 2
                                kb.w("vector", tmp_free[dp])
                                kb.op("vector", lambda e, o=tmp[dp][:, :], a=xs:
                                      e.tensor_mul(o, a, P[4][:, 0:512]))
                                dst = ln_sb[:, dc * NT + i * TTS: dc * NT + (i + 1) * TTS]
                                t_ap = kb.op("vector", lambda e, o=dst, a=tmp[dp][:, :]:
                                             e.tensor_add(o, a, P[5][:, 0:512]), inc="sv")
                                tmp_free[dp] = t_ap
                            ln_ready2[i] = kb.echo()
                            xt_free[ip] = t_ap
                            ofm_free[op_] = t_p
                            p_free[4] = ("sv", t_ap[1] + 1)
                            p_free[5] = ("sv", t_ap[1] + 1)

                        w1_free = [None, None]
                        w3_free = [None, None]
                        pt_free = [None, None]
                        for hc in range(NHC):
                            wp = hc % 2
                            t_w1 = kb.dma("gpsimd", f"w1{wp}", w1sl[wp][:, :], w1[hc], waits=[w1_free[wp]])
                            t_w3 = kb.dma("gpsimd", f"w3{wp}", w3sl[wp][:, :], w3[hc], waits=[w3_free[wp]])
                            for i in range(NTT):
                                gsel = (hc * NTT + i) % 2
                                Pg, Pl = (P[0], P[1]) if gsel == 0 else (P[2], P[3])
                                gi, li = (0, 1) if gsel == 0 else (2, 3)
                                kb.w("tensor", t_w1)
                                kb.w("tensor", t_w3)
                                kb.w("tensor", ln_ready2[i])
                                kb.w("tensor", p_free[gi])
                                kb.w("tensor", p_free[li])
                                for dc in range(NDC):
                                    kb.op("tensor", lambda e, o=Pg[:, 0:512],
                                          l=w1sl[wp][:, dc * 128:(dc + 1) * 128],
                                          r=ln_sb[:, dc * NT + i * TTS: dc * NT + (i + 1) * TTS],
                                          s=(dc == 0), t=(dc == NDC - 1):
                                          e.matmul(o, l, r, start=s, stop=t))
                                for dc in range(NDC):
                                    t_p = kb.op("tensor", lambda e, o=Pl[:, 0:512],
                                                l=w3sl[wp][:, dc * 128:(dc + 1) * 128],
                                                r=ln_sb[:, dc * NT + i * TTS: dc * NT + (i + 1) * TTS],
                                                s=(dc == 0), t=(dc == NDC - 1):
                                                e.matmul(o, l, r, start=s, stop=t),
                                                inc="sp" if dc == NDC - 1 else None)
                                if i == NTT - 1:
                                    w1_free[wp] = t_p
                                    w3_free[wp] = t_p
                                ep = i % 2
                                kb.w("scalar", t_p)
                                kb.w("scalar", ev_free[ep])
                                t_si = kb.op("scalar", lambda e, o=ev[ep][:, :], s=Pg[:, 0:512]:
                                             e.activation(o, s, AF.Silu), inc="ss")
                                p_free[gi] = ("ss", t_si[1] + 1)
                                kb.w("vector", t_si)
                                kb.w("vector", pt_free[ep])
                                t_h = kb.op("vector", lambda e, o=pt[ep][:, :], a=ev[ep][:, :],
                                            s=Pl[:, 0:512]: e.tensor_mul(o, a, s), inc="sv")
                                p_free[li] = ("sv", t_h[1] + 1)
                                ev_free[ep] = ("sv", t_h[1] + 1)
                                t_hd = kb.dma("sync", f"hd{ep}", hidd[hc * 128:(hc + 1) * 128, i * TTS:(i + 1) * TTS],
                                              pt[ep][:, :], waits=[t_h])
                                pt_free[ep] = t_hd

                        kb.barrier()

                    if stages >= 5:
                        with ExitStack() as se:
                            hblk = se.enter_context(nc.sbuf_tensor("hblk", [128, NHC * TTS], BF16))
                            w2sl = [se.enter_context(nc.sbuf_tensor(f"w2sl2{i}", [128, 64 * 128], BF16))
                                    for i in range(2)]
                            h3 = hidd.rearrange("(hc p) t -> p hc t", p=128)
                            hblk_free = None
                            w2_free = [None, None]
                            p_free = [None, None]
                            x1_free = [None, None]
                            yst_free = [None, None]
                            for i in range(NTT):
                                t_hb = kb.dma("sync", "hbl", hblk.rearrange("p (hc t) -> p hc t", t=TTS),
                                              h3[:, :, i * TTS:(i + 1) * TTS], waits=[hblk_free])
                                for oc in range(NOC):
                                    wp = oc % 2
                                    t_w = kb.dma("sync", f"wd{wp}", w2sl[wp][:, :], w2qd[oc], waits=[w2_free[wp]])
                                    xp = oc % 2
                                    t_x1 = kb.dma("sync", f"xl{xp}", x1c[xp][:, :],
                                                  x1d[oc * 128:(oc + 1) * 128, i * TTS:(i + 1) * TTS],
                                                  waits=[x1_free[xp]])
                                    pi = oc % 2
                                    kb.w("tensor", t_w)
                                    kb.w("tensor", t_hb)
                                    kb.w("tensor", p_free[pi])
                                    for a in range(NHC):
                                        t_p = kb.op("tensor", lambda e, o=P[pi][:, 0:512],
                                                    l=w2sl[wp][:, a * 128:(a + 1) * 128],
                                                    r=hblk[:, a * TTS:(a + 1) * TTS],
                                                    s=(a == 0), t=(a == NHC - 1):
                                                    e.matmul(o, l, r, start=s, stop=t),
                                                    inc="sp" if a == NHC - 1 else None)
                                    w2_free[wp] = t_p
                                    if oc == NOC - 1:
                                        hblk_free = t_p
                                    yp = oc % 2
                                    kb.w("vector", t_p)
                                    kb.w("vector", t_x1)
                                    kb.w("vector", yst_free[yp])
                                    t_y = kb.op("vector", lambda e, o=yst[yp][:, :], s=P[pi][:, 0:512],
                                                a=x1c[xp][:, :]: e.scalar_tensor_tensor(
                                                    o, s, W2SCALE, a, ALU.mult, ALU.add), inc="sv")
                                    p_free[pi] = ("sv", t_y[1] + 1)
                                    x1_free[xp] = ("sv", t_y[1] + 1)
                                    t_yo = kb.dma("sync", f"yw{yp}", y[oc * 128:(oc + 1) * 128, i * TTS:(i + 1) * TTS],
                                                  yst[yp][:, :], waits=[t_y])
                                    yst_free[yp] = t_yo

                            kb.w("sync", kb.tot("yw0"))
                            kb.w("sync", kb.tot("yw1"))

        with nc.Block() as blk:
            @blk.sync
            def _(e):
                for f in kb.ops["sync"]:
                    f(e)

            @blk.gpsimd
            def _(e):
                for f in kb.ops["gpsimd"]:
                    f(e)

            @blk.tensor
            def _(e):
                for f in kb.ops["tensor"]:
                    f(e)

            @blk.vector
            def _(e):
                for f in kb.ops["vector"]:
                    f(e)

            @blk.scalar
            def _(e):
                for f in kb.ops["scalar"]:
                    f(e)

    return nc


_NC_CACHE = None


def _get_nc():
    global _NC_CACHE
    if _NC_CACHE is None:
        _NC_CACHE = build()
    return _NC_CACHE


def prep_in_maps(L0_attn_x, L0_attn_Wq, L0_attn_Wk, L0_attn_Wv, L0_attn_Wo, L0_attn_bo,
                 L0_ffn_W1, L0_ffn_W2, L0_ffn_W3):
    xf = np.ascontiguousarray(np.asarray(L0_attn_x, np.float32))
    x2d = xf.reshape(B * L, D)

    def blockT(w, noc):
        wt = np.asarray(w, np.float32).T
        r = wt.reshape(NDC, 128, noc, 128).transpose(2, 1, 0, 3)
        return np.ascontiguousarray(r.reshape(noc, 128, NDC * 128))

    wqb = blockT(L0_attn_Wq, NOC)
    wkb = blockT(L0_attn_Wk, NOC)
    wvb = blockT(L0_attn_Wv, NOC)
    wob = blockT(L0_attn_Wo, NOC)
    w1b = blockT(L0_ffn_W1, NHC)
    w3b = blockT(L0_ffn_W3, NHC)
    w2b = np.ascontiguousarray(np.asarray(L0_ffn_W2, np.float32).T.reshape(64, 128, D))
    bob = np.ascontiguousarray(np.asarray(L0_attn_bo, np.float32).reshape(NOC, 128).T)

    in_maps = []
    for c in range(8):
        xc = np.ascontiguousarray(x2d[c * NT:(c + 1) * NT].T)
        in_maps.append({
            "x": xc, "wq": wqb, "wk": wkb, "wv": wvb, "wo": wob,
            "w1": w1b, "w3": w3b, "w2": w2b, "bo": bob,
        })
    return in_maps


def kernel(L0_attn_x, L0_attn_Wq, L0_attn_Wk, L0_attn_Wv, L0_attn_Wo, L0_attn_bo,
           L0_ffn_W1, L0_ffn_W2, L0_ffn_W3):
    in_maps = prep_in_maps(L0_attn_x, L0_attn_Wq, L0_attn_Wk, L0_attn_Wv, L0_attn_Wo,
                           L0_attn_bo, L0_ffn_W1, L0_ffn_W2, L0_ffn_W3)
    nc = _get_nc()
    res = run_bass_kernel_spmd(nc, in_maps, core_ids=list(range(8)))
    out = np.empty((B * L, D), np.float32)
    for c in range(8):
        out[c * NT:(c + 1) * NT] = np.asarray(res.results[c]["y"]).T
    return out.reshape(B, L, D)


# version 13
# speedup vs baseline: 1.2148x; 1.0083x over previous
import math
from contextlib import ExitStack

import numpy as np
import ml_dtypes

import concourse.bass as bass
from concourse.aot_env import allow_ici_all_to_all
import concourse.mybir as mybir
from concourse.bass_utils import run_bass_kernel_spmd

F32 = mybir.dt.float32
BF16 = mybir.dt.bfloat16
AF = mybir.ActivationFunctionType
ALU = mybir.AluOpType
bfdt = ml_dtypes.bfloat16

B, L, D, NH, HD, FF = 2, 8192, 2048, 16, 128, 8192
NT = 2048
TTS = 512
NTT = NT // TTS
NDC = D // 128
NOC = 16
NHC = FF // 128
EPS = 1e-5
LOG_RSHD = float(np.log(1.0 / np.sqrt(HD)))
W2SCALE = float(1.0 / np.sqrt(FF))
ENGS = ("sync", "gpsimd", "tensor", "vector", "scalar")


def _had(n):
    h = np.array([[1.0]], np.float32)
    while h.shape[0] < n:
        h = np.block([[h, h], [h, -h]])
    return h


class KB:

    def __init__(self, nc, st):
        self.nc = nc
        self.st = st
        self.ops = {e: [] for e in ENGS}
        self.sems = {}
        self.cnt = {}

    def mksem(self, *names):
        for n in names:
            if n not in self.sems:
                self.sems[n] = self.st.enter_context(self.nc.semaphore(n))
                self.cnt[n] = 0

    def w(self, eng, tok, future=0):
        if tok is None:
            return
        s, v = tok
        assert v <= self.cnt[s] + future, (eng, s, v, self.cnt[s])
        h = self.sems[s]
        self.ops[eng].append(lambda e, h=h, v=v: e.wait_ge(h, v))

    def op(self, eng, f, inc=None, amt=1):
        if inc is None:
            self.ops[eng].append(f)
            return None
        self.cnt[inc] += amt
        tok = (inc, self.cnt[inc])
        h = self.sems[inc]

        def g(e, f=f, h=h, a=amt):
            f(e).then_inc(h, a)
        self.ops[eng].append(g)
        return tok

    def dma(self, qeng, slot, dst, src, waits=(), **kw):
        for t in waits:
            self.w(qeng, t)
        sem = "d_" + slot
        self.mksem(sem)
        return self.op(qeng, lambda e, d=dst, s=src, kw=kw: e.dma_start(out=d, in_=s, **kw),
                       inc=sem, amt=16)

    def tot(self, slot):
        return ("d_" + slot, self.cnt["d_" + slot])

    def echo(self):
        return self.op("vector", lambda e: e.tensor_copy(
            self.escr[0:1, 0:1], self.escr[0:1, 1:2]), inc="sv")

    def barrier(self):
        base = self.cnt["sb"]
        for e in ENGS:
            self.op(e, lambda en: en.nop(), inc="sb")
        toks = [("sb", base + len(ENGS))]
        toks += [(s, self.cnt[s]) for s in self.sems if s.startswith("d_")]
        for e in ENGS:
            for t in toks:
                self.w(e, t)


def build(dbg=False, stages=5, p1a=True, p1m=3, p1post=3):
    nc = bass.Bass("TRN2", target_bir_lowering=False, debug=False)
    DBG = "ExternalOutput" if dbg else "Internal"

    x = nc.dram_tensor("x", [D, NT], F32, kind="ExternalInput").ap()
    y = nc.dram_tensor("y", [D, NT], F32, kind="ExternalOutput").ap()
    wq = nc.dram_tensor("wq", [NOC, 128, D], F32, kind="ExternalInput").ap()
    wk = nc.dram_tensor("wk", [NOC, 128, D], F32, kind="ExternalInput").ap()
    wv = nc.dram_tensor("wv", [NOC, 128, D], F32, kind="ExternalInput").ap()
    wo = nc.dram_tensor("wo", [NOC, 128, D], F32, kind="ExternalInput").ap()
    w1 = nc.dram_tensor("w1", [NHC, 128, D], F32, kind="ExternalInput").ap()
    w3 = nc.dram_tensor("w3", [NHC, 128, D], F32, kind="ExternalInput").ap()
    w2 = nc.dram_tensor("w2", [64, 128, D], F32, kind="ExternalInput").ap()
    bo = nc.dram_tensor("bo", [128, NOC], F32, kind="ExternalInput").ap()

    cc_qk_in = nc.dram_tensor("cc_qk_in", [2 * D, NT], BF16, kind="Internal").ap()
    cc_qk_out = nc.dram_tensor("cc_qk_out", [2 * D, NT], BF16, kind="Internal").ap()
    cc_v_in = nc.dram_tensor("cc_v_in", [D, 16 * 129], BF16, kind="Internal").ap()
    cc_vout = nc.dram_tensor("cc_vout", [D, 16 * 129], BF16, kind="Internal").ap()
    cc_o_in = nc.dram_tensor("cc_o_in", [D, NT], BF16, kind="Internal").ap()
    cc_o_out = nc.dram_tensor("cc_o_out", [D, NT], BF16, kind="Internal").ap()
    w2m = nc.dram_tensor("w2m", [64, 128 * D], BF16, kind=DBG).ap()
    w2qd = nc.dram_tensor("w2qd", [NOC, 128, 64 * 128], BF16, kind=DBG).ap()
    x1d = nc.dram_tensor("x1d", [D, NT], F32, kind=DBG).ap()
    hidd = nc.dram_tensor("hidd", [FF, NT], BF16, kind=DBG).ap()

    c_ident = nc.inline_tensor(np.eye(128, dtype=bfdt), name="c_ident").ap()
    c_h64 = nc.inline_tensor(_had(64).astype(bfdt), name="c_h64").ap()
    c_h128 = nc.inline_tensor(_had(128).astype(bfdt), name="c_h128").ap()
    tri = (np.arange(128)[:, None] <= np.arange(128)[None, :]).astype(bfdt)
    c_tri = nc.inline_tensor(tri, name="c_tri").ap()
    c_onesc = nc.inline_tensor(np.ones((128, 1), np.float32), name="c_onesc").ap()
    c_onesr = nc.inline_tensor(np.ones((1, 128), np.float32), name="c_onesr").ap()

    with ExitStack() as st:
        E = st.enter_context
        kb = KB(nc, st)
        kb.mksem("scc", "scq", "sp", "sv", "ss", "sb")

        ident = E(nc.sbuf_tensor("ident", [128, 128], BF16))
        h64s = E(nc.sbuf_tensor("h64s", [64, 64], BF16))
        h128s = E(nc.sbuf_tensor("h128s", [128, 128], BF16))
        tris = E(nc.sbuf_tensor("tris", [128, 128], BF16))
        onesc = E(nc.sbuf_tensor("onesc", [128, 1], F32))
        onesr = E(nc.sbuf_tensor("onesr", [1, 128], F32))
        bos = E(nc.sbuf_tensor("bos", [128, NOC], F32))
        r_mu = E(nc.sbuf_tensor("r_mu", [1, 512], F32))
        r_mu2 = E(nc.sbuf_tensor("r_mu2", [1, 512], F32))
        r_var = E(nc.sbuf_tensor("r_var", [1, 512], F32))
        r_lnv = E(nc.sbuf_tensor("r_lnv", [1, 512], F32))
        r_ab = E(nc.sbuf_tensor("r_ab", [1, 1024], F32))
        ev = [E(nc.sbuf_tensor(f"ev{i}", [128, 512], F32)) for i in range(2)]
        sq = [E(nc.sbuf_tensor(f"sq{i}", [128, 512], F32)) for i in range(2)]
        tmp = [E(nc.sbuf_tensor(f"tmp{i}", [128, 512], F32)) for i in range(2)]
        qkst = [E(nc.sbuf_tensor(f"qkst{i}", [128, NT], BF16)) for i in range(2)]
        vst = [E(nc.sbuf_tensor(f"vst{i}", [128, 16 * 129], BF16)) for i in range(2)]
        pt = [E(nc.sbuf_tensor(f"pt{i}", [128, 512], BF16)) for i in range(2)]
        otm = [E(nc.sbuf_tensor(f"otm{i}", [128, 128], BF16)) for i in range(2)]
        rcs = E(nc.sbuf_tensor("rcs", [128, 4], F32))
        kb_escr = E(nc.sbuf_tensor("kb_escr", [1, 2], F32))
        x1c = [E(nc.sbuf_tensor(f"x1c{i}", [128, 512], F32)) for i in range(2)]
        yst = [E(nc.sbuf_tensor(f"yst{i}", [128, 512], F32)) for i in range(2)]

        P = [E(nc.psum_tensor(f"P{i}", [128, 512], F32)) for i in range(6)]
        TPBS = [E(nc.psum_tensor(f"TPB{i}", [128, 128], BF16)) for i in range(2)]

        kb.escr = kb_escr
        for dst, src in ((ident, c_ident), (h64s, c_h64), (h128s, c_h128), (tris, c_tri),
                         (onesc, c_onesc), (onesr, c_onesr), (bos, bo)):
            kb.dma("sync", "cst", dst[:, :], src)
        t_init = kb.tot("cst")
        kb.op("vector", lambda e: e.memset(kb_escr[0:1, :], 0.0))
        for i in range(2):
            tok = kb.op("vector", lambda e, b=vst[i]: e.memset(b[:, :], 1.0), inc="sv")
        t_vst_ones = tok
        for e in ENGS:
            kb.w(e, t_init)
        kb.w("scalar", t_vst_ones)
        kb.w("sync", t_vst_ones)

        def rows_ln(stat_a, stat_b, nfeat, exp_bias, want_b):
            inv = 1.0 / nfeat
            kb.op("vector", lambda e: e.tensor_scalar_mul(r_mu[0:1, :], stat_a[0:1, 0:512], inv))
            kb.op("vector", lambda e: e.tensor_mul(r_mu2[0:1, :], r_mu[0:1, :], r_mu[0:1, :]))
            kb.op("vector", lambda e: e.scalar_tensor_tensor(
                r_var[0:1, :], stat_b[0:1, 0:512], inv, r_mu2[0:1, :], ALU.mult, ALU.subtract))
            kb.op("vector", lambda e: e.tensor_scalar_add(
                r_var[0:1, :], r_var[0:1, :], EPS))
            tv = kb.echo()
            kb.w("scalar", tv)
            tl = kb.op("scalar", lambda e: e.activation(r_lnv[0:1, :], r_var[0:1, :], AF.Ln),
                       inc="ss")
            kb.w("vector", tl)
            if exp_bias != 0.0:
                kb.op("vector", lambda e, bb=exp_bias: e.tensor_scalar(
                    r_lnv[0:1, :], r_lnv[0:1, :], -0.5, bb, ALU.mult, ALU.add))
            else:
                kb.op("vector", lambda e: e.tensor_scalar_mul(
                    r_lnv[0:1, :], r_lnv[0:1, :], -0.5))
            tm = kb.echo()
            kb.w("scalar", tm)
            ta = kb.op("scalar", lambda e: e.activation(
                r_ab[0:1, 0:512], r_lnv[0:1, :], AF.Exp), inc="ss")
            if want_b:
                kb.w("vector", ta)
                kb.op("vector", lambda e: e.tensor_mul(r_ab[0:1, 512:1024], r_mu[0:1, :],
                                                       r_ab[0:1, 0:512]))
                kb.op("vector", lambda e: e.tensor_scalar_mul(
                    r_ab[0:1, 512:1024], r_ab[0:1, 512:1024], -1.0))
                tb = kb.echo()
                return ta, tb
            return ta, None

        with ExitStack() as sa:
            ln_sb = sa.enter_context(nc.sbuf_tensor("ln_sb", [128, NDC * NT], BF16))
            xt = [sa.enter_context(nc.sbuf_tensor(f"xt{i}", [128, NDC * TTS], F32))
                  for i in range(2)]
            wsl = [sa.enter_context(nc.sbuf_tensor(f"wsl{i}", [128, D], BF16))
                   for i in range(2)]

            x3 = x.rearrange("(dc p) t -> p dc t", p=128)
            ln_ready = [None] * NTT
            xt_free = [None, None]
            rows_fr = [None]
            p_free = [None] * 6
            tpb_free = None
            ev_free = [None, None]
            sq_free = [None, None]
            pt_free = [None, None]
            tmp_free = [None, None]

            for i in range(NTT if p1a else 0):
                ip = i % 2
                t_x = kb.dma("sync", f"xt{ip}", xt[ip].rearrange("p (dc t) -> p dc t", t=TTS),
                             x3[:, :, i * TTS:(i + 1) * TTS], waits=[xt_free[ip]])
                kb.w("tensor", t_x)
                kb.w("tensor", p_free[0])
                kb.w("tensor", p_free[1])
                kb.w("vector", t_x)
                sq_toks = []
                for dc in range(NDC):
                    xs = xt[ip][:, dc * TTS:(dc + 1) * TTS]
                    kb.op("tensor", lambda e, o=P[0][0:1, 0:512], l=onesc[:, 0:1], r=xs,
                          s=(dc == 0), t=(dc == NDC - 1): e.matmul(o, l, r, start=s, stop=t))
                    dp = dc % 2
                    if dc >= 2:
                        kb.w("vector", sq_toks[dc - 2][1])
                    kb.op("vector", lambda e, o=sq[dp][:, :], a=xs: e.tensor_mul(o, a, a))
                    tsq = kb.echo()
                    kb.w("tensor", tsq)
                    tmm = kb.op("tensor", lambda e, o=P[1][0:1, 0:512], l=onesc[:, 0:1],
                                r=sq[dp][:, :], s=(dc == 0), t=(dc == NDC - 1):
                                e.matmul(o, l, r, start=s, stop=t), inc="sp")
                    sq_toks.append((tsq, tmm))
                t_stats = sq_toks[-1][1]
                kb.w("vector", t_stats)
                kb.w("vector", rows_fr[0])
                ta, tb = rows_ln(P[0], P[1], D, 0.0, True)
                kb.w("tensor", tb)
                kb.w("tensor", p_free[2])
                kb.w("tensor", p_free[3])
                kb.op("tensor", lambda e: e.matmul(P[2][:, 0:512], onesr[0:1, :],
                                                   r_ab[0:1, 0:512], start=True, stop=True))
                t_bc = kb.op("tensor", lambda e: e.matmul(P[3][:, 0:512], onesr[0:1, :],
                                                          r_ab[0:1, 512:1024], start=True,
                                                          stop=True), inc="sp")
                rows_fr[0] = t_bc
                p_free[0] = t_bc
                p_free[1] = t_bc
                kb.w("vector", t_bc)
                t_ap = None
                for dc in range(NDC):
                    xs = xt[ip][:, dc * TTS:(dc + 1) * TTS]
                    dp = dc % 2
                    if dc >= 2:
                        kb.w("vector", tmp_free[dp])
                    kb.op("vector", lambda e, o=tmp[dp][:, :], a=xs: e.tensor_mul(o, a, P[2][:, 0:512]))
                    dst = ln_sb[:, dc * NT + i * TTS: dc * NT + (i + 1) * TTS]
                    t_ap = kb.op("vector", lambda e, o=dst, a=tmp[dp][:, :]:
                                 e.tensor_add(o, a, P[3][:, 0:512]), inc="sv")
                    tmp_free[dp] = t_ap
                ln_ready[i] = kb.echo()
                xt_free[ip] = t_ap
                p_free[2] = ("sv", t_ap[1] + 1)
                p_free[3] = ("sv", t_ap[1] + 1)

            wt_free = [None, None]
            qkst_free = [None, None]
            vst_free = [None, None]
            tpb_free = [None, None]
            gtile = 0

            def post_1b(ctx):
                m, oc, i, ip, pi, t_p, op = ctx
                PP = P[pi]
                if m < 2:
                    kb.w("scalar", t_p)
                    kb.w("scalar", ev_free[ip])
                    t_ev = kb.op("scalar", lambda e, o=ev[ip][:, :], s=PP[:, 0:512]:
                                 e.copy(o, s), inc="ss")
                    p_free[pi] = ("ss", t_ev[1] + 1)
                    kb.w("vector", t_ev)
                    kb.w("vector", sq_free[ip])
                    kb.op("vector", lambda e, o=sq[ip][:, :], a=ev[ip][:, :]:
                          e.tensor_mul(o, a, a))
                    t_sq = kb.echo()
                    kb.w("tensor", t_ev)
                    kb.w("tensor", p_free[2])
                    kb.w("tensor", p_free[3])
                    kb.op("tensor", lambda e, r=ev[ip][:, :]: e.matmul(
                        P[2][0:1, 0:512], onesc[:, 0:1], r, start=True, stop=True))
                    kb.w("tensor", t_sq)
                    t_st = kb.op("tensor", lambda e, r=sq[ip][:, :]: e.matmul(
                        P[3][0:1, 0:512], onesc[:, 0:1], r, start=True, stop=True),
                        inc="sp")
                    sq_free[ip] = t_st
                    kb.w("vector", t_st)
                    kb.w("vector", rows_fr[0])
                    ta, tb = rows_ln(P[2], P[3], HD, LOG_RSHD if m == 0 else 0.0,
                                     m == 1)
                    kb.w("tensor", tb if m == 1 else ta)
                    kb.w("tensor", p_free[4])
                    kb.w("tensor", p_free[5] if m == 1 else None)
                    if m == 1:
                        kb.op("tensor", lambda e: e.matmul(
                            P[4][:, 0:512], onesr[0:1, :], r_ab[0:1, 0:512],
                            start=True, stop=True))
                        t_bc = kb.op("tensor", lambda e: e.matmul(
                            P[5][:, 0:512], onesr[0:1, :], r_ab[0:1, 512:1024],
                            start=True, stop=True), inc="sp")
                    else:
                        t_bc = kb.op("tensor", lambda e: e.matmul(
                            P[4][:, 0:512], onesr[0:1, :], r_ab[0:1, 0:512],
                            start=True, stop=True), inc="sp")
                    rows_fr[0] = t_bc
                    p_free[2] = t_bc
                    p_free[3] = t_bc
                    kb.w("vector", t_bc)
                    kb.w("vector", qkst_free[op])
                    dst = qkst[op][:, i * TTS:(i + 1) * TTS]
                    if m == 0:
                        t_sg = kb.op("vector", lambda e, o=dst, a=ev[ip][:, :]:
                                     e.tensor_mul(o, a, P[4][:, 0:512]), inc="sv")
                    else:
                        kb.w("vector", tmp_free[0])
                        kb.op("vector", lambda e, a=ev[ip][:, :]:
                              e.tensor_mul(tmp[0][:, :], a, P[4][:, 0:512]))
                        t_sg = kb.op("vector", lambda e, o=dst:
                                     e.tensor_add(o, tmp[0][:, :], P[5][:, 0:512]),
                                     inc="sv")
                        tmp_free[0] = t_sg
                    ev_free[ip] = ("sv", t_sg[1] + 1)
                    p_free[4] = ("sv", t_sg[1] + 1)
                    p_free[5] = ("sv", t_sg[1] + 1)
                    t_last = t_sg
                else:
                    kb.w("scalar", t_p)
                    kb.w("scalar", pt_free[ip])
                    t_vb = kb.op("scalar", lambda e, o=pt[ip][:, :], s=PP[:, 0:512]:
                                 e.copy(o, s), inc="ss")
                    p_free[pi] = ("ss", t_vb[1] + 1)
                    for qb in range(4):
                        tb_ = qb % 2
                        kb.w("tensor", t_vb)
                        kb.w("tensor", tpb_free[tb_], future=1)
                        t_tp = kb.op("tensor", lambda e, s=pt[ip][:, qb * 128:(qb + 1) * 128],
                                     o=TPBS[tb_][:, :]:
                                     e.transpose(o, s, ident[:, :]), inc="sp")
                        if qb == 3:
                            pt_free[ip] = t_tp
                        kb.w("scalar", t_tp)
                        kb.w("scalar", vst_free[op])
                        col = (i * 4 + qb) * 129
                        t_te = kb.op("scalar", lambda e, o=vst[op][:, col:col + 128],
                                     s=TPBS[tb_][:, :]: e.copy(o, s), inc="ss")
                        tpb_free[tb_] = ("ss", t_te[1] + 1)
                        t_last = t_te
                if i == NTT - 1:
                    if m == 0:
                        ro = (oc // 2) * 512 + (oc % 2) * 128
                        dst = cc_qk_in[ro:ro + 128, :]
                        t_out = kb.dma("sync", f"qs{op}", dst, qkst[op][:, :], waits=[t_last])
                        qkst_free[op] = t_out
                    elif m == 1:
                        ro = (oc // 2) * 512 + 256 + (oc % 2) * 128
                        dst = cc_qk_in[ro:ro + 128, :]
                        t_out = kb.dma("sync", f"qs{op}", dst, qkst[op][:, :], waits=[t_last])
                        qkst_free[op] = t_out
                    else:
                        ro = (oc // 2) * 256 + (oc % 2) * 128
                        dst = cc_v_in[ro:ro + 128, :]
                        t_out = kb.dma("sync", f"vs{op}", dst, vst[op][:, :], waits=[t_last])
                        vst_free[op] = t_out

            pending = None
            for m, wten in enumerate((wq, wk, wv)[:p1m]):
                for oc in range(NOC):
                    wp = oc % 2
                    t_w = kb.dma("gpsimd", f"ws{wp}", wsl[wp][:, :], wten[oc], waits=[wt_free[wp]])
                    op = oc % 2
                    for i in range(NTT):
                        ip = i % 2
                        pi = gtile % 2
                        gtile += 1
                        PP = P[pi]
                        kb.w("tensor", t_w)
                        kb.w("tensor", ln_ready[i])
                        kb.w("tensor", p_free[pi], future=2)
                        for dc in range(NDC):
                            t_p = kb.op("tensor", lambda e, o=PP[:, 0:512],
                                        l=wsl[wp][:, dc * 128:(dc + 1) * 128],
                                        r=ln_sb[:, dc * NT + i * TTS: dc * NT + (i + 1) * TTS],
                                        s=(dc == 0), t=(dc == NDC - 1):
                                        e.matmul(o, l, r, start=s, stop=t),
                                        inc="sp" if dc == NDC - 1 else None)
                        wt_free[wp] = t_p
                        cur = (m, oc, i, ip, pi, t_p, op)
                        if pending is not None:
                            post_1b(pending)
                        pending = cur
                    if m == 1 and oc == NOC - 1 and stages >= 2:
                        post_1b(pending)
                        pending = None
                        kb.mksem("d_qs0", "d_qs1")
                        kb.w("gpsimd", kb.tot("qs0"))
                        kb.w("gpsimd", kb.tot("qs1"))
                        t_cc_qk = kb.op("gpsimd",
                                        lambda e: a2a(e, cc_qk_in, cc_qk_out),
                                        inc="scq")
            if pending is not None:
                post_1b(pending)

            kb.mksem("d_qs0", "d_qs1", "d_vs0", "d_vs1")
            t_p1_dma = [kb.tot("qs0"), kb.tot("qs1"), kb.tot("vs0"), kb.tot("vs1")]

            if stages >= 2:
                kb.w("gpsimd", t_p1_dma[2])
                kb.w("gpsimd", t_p1_dma[3])
                t_cc_v = kb.op("gpsimd", lambda e: a2a(e, cc_v_in, cc_vout), inc="scc")

            kb.barrier()

        if stages >= 2:
            def a2a(e, i_, o_):
                with allow_ici_all_to_all():
                    return e.collective_compute(
                        "AllToAll", ALU.bypass,
                        replica_groups=[[0, 1, 2, 3, 4, 5, 6, 7]],
                        ins=[i_.opt()], outs=[o_.opt()])
            with ExitStack() as sb:
                w2sl = [sb.enter_context(nc.sbuf_tensor(f"w2sl{i}", [64, 8192], BF16))
                        for i in range(2)]
                f1st = sb.enter_context(nc.sbuf_tensor("f1st", [64, 8192], BF16))
                w2f = [sb.enter_context(nc.sbuf_tensor(f"w2f{i}", [128, D], BF16))
                       for i in range(2)]
                f2st = sb.enter_context(nc.sbuf_tensor("f2st", [128, D], BF16))
                w2flat = w2.rearrange("a b d -> a (b d)")

                sl_free = [None, None]
                p_free = [None, None]
                f1_free = None
                for s in range(32):
                    sp_ = s % 2
                    t_w = kb.dma("gpsimd", f"fw{sp_}", w2sl[sp_][:, :],
                                 w2flat[:, s * 8192:(s + 1) * 8192], waits=[sl_free[sp_]])
                    seg_toks = []
                    for n in range(16):
                        pi = n % 2
                        kb.w("tensor", t_w)
                        kb.w("tensor", p_free[pi])
                        t_mm = kb.op("tensor", lambda e, o=P[pi][0:64, 0:512],
                                     r=w2sl[sp_][:, n * 512:(n + 1) * 512]:
                                     e.matmul(o, h64s[0:64, 0:64], r, start=True, stop=True),
                                     inc="sp")
                        kb.w("scalar", t_mm)
                        kb.w("scalar", f1_free if n == 0 else None)
                        t_ev = kb.op("scalar", lambda e, o=f1st[:, n * 512:(n + 1) * 512],
                                     s2=P[pi][0:64, 0:512]:
                                     e.copy(o, s2), inc="ss")
                        p_free[pi] = t_ev
                        seg_toks.append(t_ev)
                    sl_free[sp_] = seg_toks[-1]
                    f1_free = kb.dma("sync", "f1w", w2m[:, s * 8192:(s + 1) * 8192], f1st[:, :],
                                     waits=[seg_toks[-1]])
                t_w2m = kb.tot("f1w")
                f2_free = None
                wf_free = [None, None]
                for a in range(64):
                    ap_ = a % 2
                    t_w = kb.dma("sync", f"fl{ap_}", w2f[ap_][:, :],
                                 w2m[a:a + 1, :].rearrange("o (b d) -> (o b) d", b=128),
                                 waits=[wf_free[ap_], t_w2m if a == 0 else None])
                    seg_toks = []
                    for n in range(4):
                        pi = n % 2
                        kb.w("tensor", t_w)
                        kb.w("tensor", p_free[pi])
                        t_mm = kb.op("tensor", lambda e, o=P[pi][:, 0:512],
                                     r=w2f[ap_][:, n * 512:(n + 1) * 512]:
                                     e.matmul(o, h128s[:, :], r, start=True, stop=True), inc="sp")
                        kb.w("scalar", t_mm)
                        kb.w("scalar", f2_free if n == 0 else None)
                        t_ev = kb.op("scalar", lambda e, o=f2st[:, n * 512:(n + 1) * 512],
                                     s2=P[pi][:, 0:512]: e.copy(o, s2), inc="ss")
                        p_free[pi] = ("ss", t_ev[1] + 1)
                        seg_toks.append(t_ev)
                    wf_free[ap_] = seg_toks[-1]
                    f2_free = kb.dma(
                        "sync", "f2w",
                        w2qd.rearrange("oc b f -> b oc f")[:, :, a * 128:(a + 1) * 128],
                        f2st.rearrange("b (oc o) -> b oc o", oc=16),
                        waits=[seg_toks[-1]])
                t_w2q = kb.tot("f2w")

                kb.barrier()

            if stages >= 3:
                with ExitStack() as sc:
                    kbuf = [sc.enter_context(nc.sbuf_tensor(f"kbuf{i}", [128, L], BF16))
                            for i in range(2)]
                    vbuf = [sc.enter_context(nc.sbuf_tensor(f"vbuf{i}", [128, 64 * 129], BF16))
                            for i in range(2)]
                    qbuf = [sc.enter_context(nc.sbuf_tensor(f"qbuf{i}", [128, L], BF16))
                            for i in range(2)]
                    osta = [sc.enter_context(nc.sbuf_tensor(f"osta{i}", [128, L], BF16))
                            for i in range(2)]

                    pair_cons = [None] * 4
                    osta_free = [None, None]
                    ld_toks = [None] * 4

                    qko3 = cc_qk_out.rearrange("(b j four p) t -> four b p j t",
                                               b=2, j=4, four=4)
                    vo3 = cc_vout.rearrange("(b j hl p) f -> hl b p j f", b=2, j=4, hl=2)

                    def emit_pair_load(p):
                        hl, b = p // 2, p % 2
                        bp = p % 2
                        w_ = [pair_cons[p - 2] if p >= 2 else None]
                        t1 = kb.dma("sync", f"kb{bp}",
                                    kbuf[bp].rearrange("p (j t) -> p j t", j=4),
                                    qko3[2 + hl, b], waits=w_)
                        t2 = kb.dma("sync", f"vb{bp}",
                                    vbuf[bp].rearrange("p (j f) -> p j f", j=4),
                                    vo3[hl, b])
                        t3 = kb.dma("sync", f"qb{bp}",
                                    qbuf[bp].rearrange("p (j t) -> p j t", j=4),
                                    qko3[hl, b])
                        ld_toks[p] = (t1, t2, t3)

                    kb.w("sync", t_cc_qk)
                    kb.w("sync", t_cc_v)
                    emit_pair_load(0)
                    emit_pair_load(1)
                    oin = cc_o_in.rearrange("(b j hl p) t -> b hl p j t", b=2, j=4, hl=2)

                    pst_free = [None, None]
                    pt_free = [None, None]
                    acc_free = [None] * 4
                    tpb_free = [None, None]
                    otm_free = [None, None]

                    for p in range(4):
                        hl, b = p // 2, p % 2
                        bp = p % 2
                        for t in ld_toks[p]:
                            kb.w("tensor", t)
                        t_pe_last = None
                        for qg in range(16):
                            qbase = qg * 512
                            nkc = 4 * qg + 4
                            stop_tok = [None] * 4
                            st_toks = [None] * nkc

                            def emit_st(kc, qg=qg, qbase=qbase, nkc=nkc):
                                par = kc % 2
                                diag = kc >= 4 * qg
                                qoff = (kc - 4 * qg) * 128 if diag else 0
                                mov = 512 - qoff
                                kb.w("tensor", pst_free[par], future=1)
                                st_toks[kc] = kb.op("tensor", lambda e, o=P[par][:, 0:mov],
                                                    l=kbuf[bp][:, kc * 128:(kc + 1) * 128],
                                                    r=qbuf[bp][:, qbase + qoff: qbase + 512]:
                                                    e.matmul(o, l, r, start=True, stop=True),
                                                    inc="sp")

                            emit_st(0)
                            for kc in range(nkc):
                                par = kc % 2
                                diag = kc >= 4 * qg
                                qoff = (kc - 4 * qg) * 128 if diag else 0
                                mov = 512 - qoff
                                if kc + 1 < nkc:
                                    emit_st(kc + 1)
                                t_st = st_toks[kc]
                                kb.w("scalar", t_st)
                                kb.w("scalar", pt_free[par])
                                t_exp = kb.op("scalar", lambda e, o=pt[par][:, 0:mov],
                                              s=P[par][:, 0:mov]:
                                              e.activation(o, s, AF.Exp), inc="ss")
                                pst_free[par] = ("ss", t_exp[1] + 1)
                                if diag:
                                    kb.w("vector", t_exp)
                                    kb.op("vector", lambda e, o=pt[par][:, 0:128]:
                                          e.tensor_mul(o, o, tris[:, :]))
                                    t_msk = kb.echo()
                                    kb.w("tensor", t_msk)
                                else:
                                    kb.w("tensor", t_exp)
                                qb0 = (kc - 4 * qg) if diag else 0
                                for qb in range(qb0, 4):
                                    if kc == 0:
                                        kb.w("tensor", acc_free[qb])
                                    last = qb == 3
                                    t_pv = kb.op("tensor", lambda e, o=P[2 + qb][:, 0:129],
                                                 l=pt[par][:, qb * 128 - qoff: qb * 128 - qoff + 128],
                                                 r=vbuf[bp][:, kc * 129:(kc + 1) * 129],
                                                 s=(kc == 0), t=(kc == 4 * qg + qb):
                                                 e.matmul(o, l, r, start=s, stop=t),
                                                 inc="sp" if last else None)
                                for qb in range(qb0, 4):
                                    if kc == 4 * qg + qb:
                                        stop_tok[qb] = t_pv
                                pt_free[par] = t_pv
                                t_pe_last = t_pv
                            for qb in range(4):
                                po = qb % 2
                                tb_ = qb % 2
                                kb.w("vector", stop_tok[qb])
                                kb.op("vector", lambda e, s=P[2 + qb][:, 128:129], o=rcs[:, qb:qb + 1]:
                                      e.reciprocal(o, s))
                                t_rc = kb.echo()
                                kb.w("scalar", t_rc)
                                kb.w("scalar", otm_free[po])
                                t_nm = kb.op("scalar", lambda e, o=otm[po][:, :],
                                             s=P[2 + qb][:, 0:128], r=rcs[:, qb:qb + 1]:
                                             e.activation(o, s, AF.Copy, scale=r), inc="ss")
                                kb.w("tensor", t_nm)
                                kb.w("tensor", tpb_free[tb_])
                                t_tp = kb.op("tensor", lambda e, s=otm[po][:, :],
                                             o=TPBS[tb_][:, :]:
                                             e.transpose(o, s, ident[:, :]), inc="sp")
                                kb.w("scalar", t_tp)
                                kb.w("scalar", osta_free[bp] if (qg == 0 and qb == 0) else None)
                                col = qbase + qb * 128
                                t_te = kb.op("scalar", lambda e, o=osta[bp][:, col:col + 128],
                                             s=TPBS[tb_][:, :]: e.copy(o, s), inc="ss")
                                acc_free[qb] = t_te
                                tpb_free[tb_] = ("ss", t_te[1] + 1)
                                otm_free[po] = t_tp
                        pair_cons[p] = t_pe_last
                        t_o = kb.dma("sync", f"ost{bp}", oin[b, hl],
                                     osta[bp].rearrange("p (j t) -> p j t", j=4),
                                     waits=[t_te])
                        osta_free[bp] = t_o
                        if p + 2 < 4:
                            emit_pair_load(p + 2)

                    kb.w("gpsimd", kb.tot("ost0"))
                    kb.w("gpsimd", kb.tot("ost1"))
                    t_cc_o = kb.op("gpsimd", lambda e: a2a(e, cc_o_in, cc_o_out), inc="scc")
                    kb.barrier()

                if stages >= 4:
                    with ExitStack() as sd_:
                        ln_sb = sd_.enter_context(nc.sbuf_tensor("ln2_sb", [128, NDC * NT], BF16))
                        xt = [sd_.enter_context(nc.sbuf_tensor(f"xt4{i}", [128, NDC * TTS], F32))
                              for i in range(1)]
                        ofm = [sd_.enter_context(nc.sbuf_tensor(f"ofm{i}", [128, NOC * TTS], BF16))
                               for i in range(2)]
                        wsl = [sd_.enter_context(nc.sbuf_tensor(f"wsl4{i}", [128, D], BF16))
                               for i in range(2)]
                        w1sl = [sd_.enter_context(nc.sbuf_tensor(f"w1sl{i}", [128, D], BF16))
                                for i in range(2)]
                        w3sl = [sd_.enter_context(nc.sbuf_tensor(f"w3sl{i}", [128, D], BF16))
                                for i in range(2)]

                        x3 = x.rearrange("(dc p) t -> p dc t", p=128)
                        mo3 = cc_o_out.rearrange("(h p) t -> p h t", p=128)
                        ln_ready2 = [None] * NTT
                        p_free = [None] * 6
                        xt_free = [None]
                        ofm_free = [None, None]
                        wt_free = [None, None]
                        ev_free = [None, None]
                        sq_free = [None, None]
                        tmp_free = [None, None]
                        rows_free = None
                        t_x1st = None
                        for i in range(NTT):
                            ip = 0
                            op_ = i % 2
                            t_x = kb.dma("sync", "xt0", xt[ip].rearrange("p (dc t) -> p dc t", t=TTS),
                                         x3[:, :, i * TTS:(i + 1) * TTS], waits=[xt_free[ip], t_x1st])
                            t_of = kb.dma("sync", f"of{op_}", ofm[op_].rearrange("p (h t) -> p h t", t=TTS),
                                          mo3[:, :, i * TTS:(i + 1) * TTS],
                                          waits=[ofm_free[op_], t_cc_o])
                            x1_toks = []
                            for oc in range(NOC):
                                wp = oc % 2
                                t_w = kb.dma("gpsimd", f"ws{wp}", wsl[wp][:, :], wo[oc], waits=[wt_free[wp]])
                                pi = oc % 2
                                kb.w("tensor", t_w)
                                kb.w("tensor", t_of)
                                kb.w("tensor", p_free[pi])
                                for dc in range(NDC):
                                    t_p = kb.op("tensor", lambda e, o=P[pi][:, 0:512],
                                                l=wsl[wp][:, dc * 128:(dc + 1) * 128],
                                                r=ofm[op_][:, dc * TTS:(dc + 1) * TTS],
                                                s=(dc == 0), t=(dc == NDC - 1):
                                                e.matmul(o, l, r, start=s, stop=t),
                                                inc="sp" if dc == NDC - 1 else None)
                                wt_free[wp] = t_p
                                kb.w("vector", t_p)
                                kb.w("vector", t_x)
                                xs = xt[ip][:, oc * TTS:(oc + 1) * TTS]
                                kb.op("vector", lambda e, o=xs, s=P[pi][:, 0:512],
                                      bb=bos[:, oc:oc + 1]:
                                      e.scalar_tensor_tensor(o, s, bb, o, ALU.add, ALU.add))
                                sp2 = oc % 2
                                kb.w("vector", sq_free[sp2])
                                kb.op("vector", lambda e, o=sq[sp2][:, :], a=xs:
                                      e.tensor_mul(o, a, a))
                                t_x1 = kb.echo()
                                p_free[pi] = t_x1
                                x1_toks.append(t_x1)
                                t_x1st = kb.dma("sync", "x1s", x1d[oc * 128:(oc + 1) * 128, i * TTS:(i + 1) * TTS],
                                                xs, waits=[t_x1])
                                kb.w("tensor", t_x1)
                                if oc == 0:
                                    kb.w("tensor", p_free[2])
                                    kb.w("tensor", p_free[3])
                                kb.op("tensor", lambda e, r=xs, s=(oc == 0), t=(oc == NOC - 1):
                                      e.matmul(P[2][0:1, 0:512], onesc[:, 0:1], r, start=s, stop=t))
                                t_st = kb.op("tensor", lambda e, r=sq[sp2][:, :], s=(oc == 0),
                                             t=(oc == NOC - 1):
                                             e.matmul(P[3][0:1, 0:512], onesc[:, 0:1], r, start=s, stop=t),
                                             inc="sp")
                                sq_free[sp2] = t_st
                            t_stats = t_st
                            kb.w("vector", t_stats)
                            kb.w("vector", rows_free)
                            ta, tb = rows_ln(P[2], P[3], D, 0.0, True)
                            kb.w("tensor", tb)
                            kb.w("tensor", p_free[4])
                            kb.w("tensor", p_free[5])
                            kb.op("tensor", lambda e: e.matmul(P[4][:, 0:512], onesr[0:1, :],
                                                               r_ab[0:1, 0:512], start=True, stop=True))
                            t_bc = kb.op("tensor", lambda e: e.matmul(P[5][:, 0:512], onesr[0:1, :],
                                                                      r_ab[0:1, 512:1024], start=True,
                                                                      stop=True), inc="sp")
                            rows_free = t_bc
                            p_free[2] = t_bc
                            p_free[3] = t_bc
                            kb.w("vector", t_bc)
                            t_ap = None
                            for dc in range(NDC):
                                xs = xt[ip][:, dc * TTS:(dc + 1) * TTS]
                                dp = dc % 2
                                kb.w("vector", tmp_free[dp])
                                kb.op("vector", lambda e, o=tmp[dp][:, :], a=xs:
                                      e.tensor_mul(o, a, P[4][:, 0:512]))
                                dst = ln_sb[:, dc * NT + i * TTS: dc * NT + (i + 1) * TTS]
                                t_ap = kb.op("vector", lambda e, o=dst, a=tmp[dp][:, :]:
                                             e.tensor_add(o, a, P[5][:, 0:512]), inc="sv")
                                tmp_free[dp] = t_ap
                            ln_ready2[i] = kb.echo()
                            xt_free[ip] = t_ap
                            ofm_free[op_] = t_p
                            p_free[4] = ("sv", t_ap[1] + 1)
                            p_free[5] = ("sv", t_ap[1] + 1)

                        w1_free = [None, None]
                        w3_free = [None, None]
                        pt_free = [None, None]
                        for hc in range(NHC):
                            wp = hc % 2
                            t_w1 = kb.dma("gpsimd", f"w1{wp}", w1sl[wp][:, :], w1[hc], waits=[w1_free[wp]])
                            t_w3 = kb.dma("gpsimd", f"w3{wp}", w3sl[wp][:, :], w3[hc], waits=[w3_free[wp]])
                            for i in range(NTT):
                                gsel = (hc * NTT + i) % 2
                                Pg, Pl = (P[0], P[1]) if gsel == 0 else (P[2], P[3])
                                gi, li = (0, 1) if gsel == 0 else (2, 3)
                                kb.w("tensor", t_w1)
                                kb.w("tensor", t_w3)
                                kb.w("tensor", ln_ready2[i])
                                kb.w("tensor", p_free[gi])
                                kb.w("tensor", p_free[li])
                                for dc in range(NDC):
                                    kb.op("tensor", lambda e, o=Pg[:, 0:512],
                                          l=w1sl[wp][:, dc * 128:(dc + 1) * 128],
                                          r=ln_sb[:, dc * NT + i * TTS: dc * NT + (i + 1) * TTS],
                                          s=(dc == 0), t=(dc == NDC - 1):
                                          e.matmul(o, l, r, start=s, stop=t))
                                for dc in range(NDC):
                                    t_p = kb.op("tensor", lambda e, o=Pl[:, 0:512],
                                                l=w3sl[wp][:, dc * 128:(dc + 1) * 128],
                                                r=ln_sb[:, dc * NT + i * TTS: dc * NT + (i + 1) * TTS],
                                                s=(dc == 0), t=(dc == NDC - 1):
                                                e.matmul(o, l, r, start=s, stop=t),
                                                inc="sp" if dc == NDC - 1 else None)
                                if i == NTT - 1:
                                    w1_free[wp] = t_p
                                    w3_free[wp] = t_p
                                ep = i % 2
                                kb.w("scalar", t_p)
                                kb.w("scalar", ev_free[ep])
                                t_si = kb.op("scalar", lambda e, o=ev[ep][:, :], s=Pg[:, 0:512]:
                                             e.activation(o, s, AF.Silu), inc="ss")
                                p_free[gi] = ("ss", t_si[1] + 1)
                                kb.w("vector", t_si)
                                kb.w("vector", pt_free[ep])
                                t_h = kb.op("vector", lambda e, o=pt[ep][:, :], a=ev[ep][:, :],
                                            s=Pl[:, 0:512]: e.tensor_mul(o, a, s), inc="sv")
                                p_free[li] = ("sv", t_h[1] + 1)
                                ev_free[ep] = ("sv", t_h[1] + 1)
                                t_hd = kb.dma("sync", f"hd{ep}", hidd[hc * 128:(hc + 1) * 128, i * TTS:(i + 1) * TTS],
                                              pt[ep][:, :], waits=[t_h])
                                pt_free[ep] = t_hd

                        kb.barrier()

                    if stages >= 5:
                        with ExitStack() as se:
                            hblk = se.enter_context(nc.sbuf_tensor("hblk", [128, NHC * TTS], BF16))
                            w2sl = [se.enter_context(nc.sbuf_tensor(f"w2sl2{i}", [128, 64 * 128], BF16))
                                    for i in range(2)]
                            h3 = hidd.rearrange("(hc p) t -> p hc t", p=128)
                            hblk_free = None
                            w2_free = [None, None]
                            p_free = [None, None]
                            x1_free = [None, None]
                            yst_free = [None, None]
                            for i in range(NTT):
                                t_hb = kb.dma("sync", "hbl", hblk.rearrange("p (hc t) -> p hc t", t=TTS),
                                              h3[:, :, i * TTS:(i + 1) * TTS], waits=[hblk_free])
                                for oc in range(NOC):
                                    wp = oc % 2
                                    t_w = kb.dma("sync", f"wd{wp}", w2sl[wp][:, :], w2qd[oc], waits=[w2_free[wp]])
                                    xp = oc % 2
                                    t_x1 = kb.dma("sync", f"xl{xp}", x1c[xp][:, :],
                                                  x1d[oc * 128:(oc + 1) * 128, i * TTS:(i + 1) * TTS],
                                                  waits=[x1_free[xp]])
                                    pi = oc % 2
                                    kb.w("tensor", t_w)
                                    kb.w("tensor", t_hb)
                                    kb.w("tensor", p_free[pi])
                                    for a in range(NHC):
                                        t_p = kb.op("tensor", lambda e, o=P[pi][:, 0:512],
                                                    l=w2sl[wp][:, a * 128:(a + 1) * 128],
                                                    r=hblk[:, a * TTS:(a + 1) * TTS],
                                                    s=(a == 0), t=(a == NHC - 1):
                                                    e.matmul(o, l, r, start=s, stop=t),
                                                    inc="sp" if a == NHC - 1 else None)
                                    w2_free[wp] = t_p
                                    if oc == NOC - 1:
                                        hblk_free = t_p
                                    yp = oc % 2
                                    kb.w("vector", t_p)
                                    kb.w("vector", t_x1)
                                    kb.w("vector", yst_free[yp])
                                    t_y = kb.op("vector", lambda e, o=yst[yp][:, :], s=P[pi][:, 0:512],
                                                a=x1c[xp][:, :]: e.scalar_tensor_tensor(
                                                    o, s, W2SCALE, a, ALU.mult, ALU.add), inc="sv")
                                    p_free[pi] = ("sv", t_y[1] + 1)
                                    x1_free[xp] = ("sv", t_y[1] + 1)
                                    t_yo = kb.dma("sync", f"yw{yp}", y[oc * 128:(oc + 1) * 128, i * TTS:(i + 1) * TTS],
                                                  yst[yp][:, :], waits=[t_y])
                                    yst_free[yp] = t_yo

                            kb.w("sync", kb.tot("yw0"))
                            kb.w("sync", kb.tot("yw1"))

        with nc.Block() as blk:
            @blk.sync
            def _(e):
                for f in kb.ops["sync"]:
                    f(e)

            @blk.gpsimd
            def _(e):
                for f in kb.ops["gpsimd"]:
                    f(e)

            @blk.tensor
            def _(e):
                for f in kb.ops["tensor"]:
                    f(e)

            @blk.vector
            def _(e):
                for f in kb.ops["vector"]:
                    f(e)

            @blk.scalar
            def _(e):
                for f in kb.ops["scalar"]:
                    f(e)

    return nc


_NC_CACHE = None


def _get_nc():
    global _NC_CACHE
    if _NC_CACHE is None:
        _NC_CACHE = build()
    return _NC_CACHE


def prep_in_maps(L0_attn_x, L0_attn_Wq, L0_attn_Wk, L0_attn_Wv, L0_attn_Wo, L0_attn_bo,
                 L0_ffn_W1, L0_ffn_W2, L0_ffn_W3):
    xf = np.ascontiguousarray(np.asarray(L0_attn_x, np.float32))
    x2d = xf.reshape(B * L, D)

    def blockT(w, noc):
        wt = np.asarray(w, np.float32).T
        r = wt.reshape(NDC, 128, noc, 128).transpose(2, 1, 0, 3)
        return np.ascontiguousarray(r.reshape(noc, 128, NDC * 128))

    wqb = blockT(L0_attn_Wq, NOC)
    wkb = blockT(L0_attn_Wk, NOC)
    wvb = blockT(L0_attn_Wv, NOC)
    wob = blockT(L0_attn_Wo, NOC)
    w1b = blockT(L0_ffn_W1, NHC)
    w3b = blockT(L0_ffn_W3, NHC)
    w2b = np.ascontiguousarray(np.asarray(L0_ffn_W2, np.float32).T.reshape(64, 128, D))
    bob = np.ascontiguousarray(np.asarray(L0_attn_bo, np.float32).reshape(NOC, 128).T)

    in_maps = []
    for c in range(8):
        xc = np.ascontiguousarray(x2d[c * NT:(c + 1) * NT].T)
        in_maps.append({
            "x": xc, "wq": wqb, "wk": wkb, "wv": wvb, "wo": wob,
            "w1": w1b, "w3": w3b, "w2": w2b, "bo": bob,
        })
    return in_maps


def kernel(L0_attn_x, L0_attn_Wq, L0_attn_Wk, L0_attn_Wv, L0_attn_Wo, L0_attn_bo,
           L0_ffn_W1, L0_ffn_W2, L0_ffn_W3):
    in_maps = prep_in_maps(L0_attn_x, L0_attn_Wq, L0_attn_Wk, L0_attn_Wv, L0_attn_Wo,
                           L0_attn_bo, L0_ffn_W1, L0_ffn_W2, L0_ffn_W3)
    nc = _get_nc()
    res = run_bass_kernel_spmd(nc, in_maps, core_ids=list(range(8)))
    out = np.empty((B * L, D), np.float32)
    for c in range(8):
        out[c * NT:(c + 1) * NT] = np.asarray(res.results[c]["y"]).T
    return out.reshape(B, L, D)
